# version 16
# speedup vs baseline: 1.1859x; 1.0076x over previous
import math
from contextlib import ExitStack

import numpy as np
import ml_dtypes

import concourse.bass as bass
import concourse.mybir as mybir
from concourse.bass_utils import run_bass_kernel_spmd

F32 = mybir.dt.float32
BF16 = mybir.dt.bfloat16
I32 = mybir.dt.int32
AF = mybir.ActivationFunctionType
ALU = mybir.AluOpType
AX = mybir.AxisListType.X
BF = ml_dtypes.bfloat16

B, T, D = 4, 2048, 4096
NCORES = 8
TME = 1024
RS2 = 0.08838834764831845
LNEPS = 1e-5

_NC_CACHE = {}
ENGS = ("sync", "vector", "scalar", "tensor", "gpsimd")


class KSem:
    def __init__(self, name):
        self.name = name
        self.h = None
        self.n = 0


class KB:

    DQ = ("sync", "scalar", "gpsimd", "vector")

    def __init__(self, nc):
        self.nc = nc
        self.ops = {e: [] for e in ENGS}
        self.sems = []
        self._lastwait = {e: {} for e in ENGS}
        self.dsem = {q: self.sem("dq_" + q) for q in self.DQ}
        self.self_sems = {}
        self.flush_all = False

    def set_self(self, eng, sem):
        self.self_sems[eng] = sem

    def sem(self, name):
        s = KSem(name)
        self.sems.append(s)
        return s

    def mark(self):
        return [(s, s.n) for s in self.dsem.values() if s.n > 0]

    def emit(self, eng, fn, waits=(), inc=None, dma_op=False, flush=False):
        selfsem = None if dma_op else self.self_sems.get(eng)
        if selfsem is not None:
            if flush or self.flush_all:
                waits = list(waits) + [(selfsem, selfsem.n)]
            if inc is None:
                inc = selfsem
            else:
                s0, _ = (inc, 1) if isinstance(inc, KSem) else inc
                assert s0 is selfsem, f"op on {eng} must inc its self-sem"
        wl = []
        for w in waits:
            if w == "dma":
                items = [(s, s.n) for s in self.dsem.values()]
            elif isinstance(w, KSem):
                items = [(w, w.n)]
            else:
                items = [w]
            for s, v in items:
                if v <= 0:
                    continue
                lw = self._lastwait[eng].get(s.name, -1)
                if v > lw:
                    self._lastwait[eng][s.name] = v
                    wl.append((s, v))
        if inc is not None:
            s, k = (inc, 1) if isinstance(inc, KSem) else inc
            s.n += k
            incs = (s, k)
        else:
            incs = None
        self.ops[eng].append((fn, wl, incs))

    def dma(self, eng, out, in_, waits=(), transpose=False):
        s = self.dsem[eng]
        if transpose:
            self.emit(eng, lambda e: e.dma_start(out=out, in_=in_, transpose=True), waits, (s, 16),
                      dma_op=True)
        else:
            self.emit(eng, lambda e: e.dma_start(out=out, in_=in_), waits, (s, 16), dma_op=True)

    def idma(self, out, in_ap, idx_ap, waits=(), element_offset=0):
        from concourse.bass_core_rust import IndirectOffsetOnAxis
        s = self.dsem["gpsimd"]
        self.emit("gpsimd", lambda e: e.indirect_dma_start(
            out=out, out_offset=None, in_=in_ap,
            in_offset=IndirectOffsetOnAxis(idx_ap, 0), element_offset=element_offset),
            waits, (s, 16), dma_op=True)

    def build(self, es):
        for s in self.sems:
            s.h = es.enter_context(self.nc.semaphore(s.name))
        block = es.enter_context(self.nc.Block())
        for eng in ENGS:
            ops = self.ops[eng]
            if not ops:
                continue

            def run(e, ops=ops):
                for fn, wl, incs in ops:
                    for s, v in wl:
                        e.wait_ge(s.h, v)
                    inst = fn(e)
                    if incs is not None:
                        s, k = incs
                        assert inst is not None
                        inst.then_inc(s.h, k)

            getattr(block, eng)(run)


def _consts(nc):
    c = {}
    c["identb"] = nc.inline_tensor(np.eye(128, dtype=BF), name="c_identb").ap()
    c["identf"] = nc.inline_tensor(np.eye(128, dtype=np.float32), name="c_identf").ap()
    m = np.zeros((128, 128), np.float32)
    m[np.triu_indices(128, 1)] = -3.0e38
    c["mask"] = nc.inline_tensor(m, name="c_mask").ap()
    c["ones1"] = nc.inline_tensor(np.ones((1, 128), np.float32), name="c_ones1").ap()
    c["onescol"] = nc.inline_tensor(np.ones((128, 1), np.float32), name="c_onescol").ap()
    c["eps"] = nc.inline_tensor(np.full((128, 1), 1e-5, np.float32), name="c_eps").ap()
    return c


def _ln_stream(kb, nc, tme, bufs, sems, z_tile_of, out_dram, z_d):
    sv, ss, stt = sems
    sq, mu, var, rstd = bufs["sq"], bufs["mu"], bufs["var"], bufs["rstd"]
    murep, rsrep = bufs["murep"], bufs["rsrep"]
    ones1_s = bufs["ones1_s"]
    ps_st, ps_st2 = bufs["ps_st"], bufs["ps_st2"]
    ps_rep = bufs["ps_rep"]
    ztl = bufs["ztl"]
    nq = tme // 512

    kb.emit("vector", lambda e: e.tensor_scalar_mul(mu[:, 0:tme], ps_st[:, 0:tme], 1.0 / D),
            waits=[stt], inc=sv)
    kb.emit("vector", lambda e: e.tensor_scalar_mul(var[:, 0:tme], ps_st2[:, 0:tme], 1.0 / D), inc=sv)
    kb.emit("vector", lambda e: e.tensor_tensor(out=sq[0:1, 0:tme], in0=mu[:, 0:tme], in1=mu[:, 0:tme],
                                                op=ALU.mult), inc=sv)
    kb.emit("vector", lambda e: e.tensor_tensor(out=var[:, 0:tme], in0=var[:, 0:tme], in1=sq[0:1, 0:tme],
                                                op=ALU.subtract), inc=sv)
    eps_s = bufs["eps_s"]
    kb.emit("scalar", lambda e: e.activation(out=var[:, 0:tme], in_=var[:, 0:tme], func=AF.Ln,
                                             bias=eps_s[0:1, 0:1]), waits=[sv], inc=ss)
    kb.emit("scalar", lambda e: e.activation(out=rstd[:, 0:tme], in_=var[:, 0:tme], func=AF.Exp, scale=-0.5),
            inc=ss)
    kb.emit("vector", lambda e: e.tensor_scalar_mul(mu[:, 0:tme], mu[:, 0:tme], -1.0), waits=[ss], inc=sv)

    def mrep1(e):
        for tc in range(nq):
            i = e.matmul(ps_rep[:, tc * 512:(tc + 1) * 512], ones1_s[:], mu[0:1, tc * 512:(tc + 1) * 512],
                         start=True, stop=True)
        return i
    kb.emit("tensor", mrep1, waits=[sv, ss], inc=stt)
    kb.emit("scalar", lambda e: e.copy(out=murep[:, 0:tme], in_=ps_rep[:, 0:tme]), waits=[stt], inc=ss)

    def mrep2(e):
        for tc in range(nq):
            i = e.matmul(ps_rep[:, tc * 512:(tc + 1) * 512], ones1_s[:], rstd[0:1, tc * 512:(tc + 1) * 512],
                         start=True, stop=True)
        return i
    kb.emit("tensor", mrep2, waits=[ss], inc=stt)
    kb.emit("scalar", lambda e: e.copy(out=rsrep[:, 0:tme], in_=ps_rep[:, 0:tme]), waits=[stt], inc=ss)

    for dt_ in range(32):
        kb.dma("sync", ztl[:, 0:tme], z_d.ap()[dt_ * 128:(dt_ + 1) * 128, :], waits=[sv, ss, "dma"])
        kb.emit("vector", lambda e: e.tensor_tensor(out=ztl[:, 0:tme], in0=ztl[:, 0:tme],
                                                    in1=murep[:, 0:tme], op=ALU.add), waits=["dma", ss], inc=sv)
        kb.emit("vector", lambda e: e.tensor_tensor(out=ztl[:, 0:tme], in0=ztl[:, 0:tme],
                                                    in1=rsrep[:, 0:tme], op=ALU.mult), inc=sv)
        kb.dma("sync", out_dram[dt_ * 128:(dt_ + 1) * 128, :], ztl[:, 0:tme], waits=[sv])


def build_attn():
    nc = bass.Bass("TRN2", target_bir_lowering=False, debug=False)
    xb = nc.dram_tensor("xb", [T, D], F32, kind="ExternalInput").ap()
    wq = nc.dram_tensor("wq", [2048, D], F32, kind="ExternalInput").ap()
    wqb = nc.dram_tensor("wqb", [128, 16], F32, kind="ExternalInput").ap()
    wk = nc.dram_tensor("wk", [128, D], F32, kind="ExternalInput").ap()
    wkb = nc.dram_tensor("wkb", [128, 1], F32, kind="ExternalInput").ap()
    wv = nc.dram_tensor("wv", [128, D], F32, kind="ExternalInput").ap()
    wvb = nc.dram_tensor("wvb", [128, 1], F32, kind="ExternalInput").ap()
    attn_o = nc.dram_tensor("attn", [2048, T], BF16, kind="ExternalOutput").ap()

    xbf = nc.dram_tensor("xbf", [T, D], BF16)
    wqbf = nc.dram_tensor("wqbf", [2048, D], BF16)
    wkbf = nc.dram_tensor("wkbf", [128, D], BF16)
    wvbf = nc.dram_tensor("wvbf", [128, D], BF16)
    cs = _consts(nc)

    kb = KB(nc)
    sv, ss, stt = (kb.sem(x) for x in ("sv", "ss", "stt"))
    kb.set_self("vector", sv)
    kb.set_self("scalar", ss)

    es = ExitStack()
    with es:
        E = lambda n, sh, dt: es.enter_context(nc.sbuf_tensor(n, sh, dt))
        xT = E("xT", [128, 32 * 512], BF16)
        wqT = E("wqT", [128, 32 * 512], BF16)
        wkT = E("wkT", [128, 32 * 128], BF16)
        wvT = E("wvT", [128, 32 * 128], BF16)
        q_sb = E("q_sb", [128, 16 * 2048], BF16)
        k_sb = E("k_sb", [128, 2048], BF16)
        v_sb = E("v_sb", [128, 2048], BF16)
        vkd = E("vkd", [128, 2048], BF16)
        p_sb = E("p_sb", [128, 2048], BF16)
        pt8 = E("pt8", [128, 1024], BF16)
        att_sb = E("att_sb", [128, 128], BF16)
        mask_s = E("mask_s", [128, 128], F32)
        identb_s = E("identb_s", [128, 128], BF16)
        nmax = E("nmax", [128, 1], F32)
        lrow = E("lrow", [128, 1], F32)
        rl = E("rl", [128, 1], F32)
        qb_s = E("qb_s", [128, 16], F32)
        kb_s = E("kb_s", [128, 1], F32)
        vb_s = E("vb_s", [128, 1], F32)

        ps_pj = es.enter_context(nc.psum_tensor("ps_pj", [128, 512], F32))
        ps_pj2 = es.enter_context(nc.psum_tensor("ps_pj2", [128, 512], F32))
        ps_s = es.enter_context(nc.psum_tensor("ps_s", [128, 2048], F32))
        ps_pt = es.enter_context(nc.psum_tensor("ps_pt", [128, 1024], BF16))
        ps_att = es.enter_context(nc.psum_tensor("ps_att", [128, 128], F32))

        for i in range(4):
            kb.dma("gpsimd", xbf.ap()[i * 512:(i + 1) * 512, :], xb[i * 512:(i + 1) * 512, :])
            kb.dma("gpsimd", wqbf.ap()[i * 512:(i + 1) * 512, :], wq[i * 512:(i + 1) * 512, :])
        kb.dma("gpsimd", wkbf.ap(), wk)
        kb.dma("gpsimd", wvbf.ap(), wv)
        n_cast = kb.mark()

        kb.dma("sync", mask_s[:], cs["mask"])
        kb.dma("sync", identb_s[:], cs["identb"])
        kb.dma("sync", qb_s[:], wqb)
        kb.dma("sync", kb_s[:], wkb)
        kb.dma("sync", vb_s[:], wvb)
        for dt_ in range(32):
            kb.dma("sync", wkT[:, dt_ * 128:(dt_ + 1) * 128], wkbf.ap()[:, dt_ * 128:(dt_ + 1) * 128],
                   waits=n_cast, transpose=True)
            kb.dma("sync", wvT[:, dt_ * 128:(dt_ + 1) * 128], wvbf.ap()[:, dt_ * 128:(dt_ + 1) * 128],
                   waits=n_cast, transpose=True)
        kb.emit("vector", lambda e: e.tensor_scalar_mul(kb_s[:], kb_s[:], RS2), waits=["dma"], inc=sv)

        for tch in range(4):
            t0 = tch * 512
            for dt_ in range(32):
                kb.dma("sync", xT[:, dt_ * 512:(dt_ + 1) * 512],
                       xbf.ap()[t0:t0 + 512, dt_ * 128:(dt_ + 1) * 128],
                       waits=[*n_cast, stt], transpose=True)

            def mmk(e):
                for dt_ in range(32):
                    i = e.matmul(ps_pj[:], wkT[:, dt_ * 128:(dt_ + 1) * 128],
                                 xT[:, dt_ * 512:(dt_ + 1) * 512], start=(dt_ == 0), stop=(dt_ == 31))
                return i
            kb.emit("tensor", mmk, waits=["dma", sv, ss], inc=stt)
            kb.emit("scalar", lambda e, t0=t0: e.activation(
                out=k_sb[:, t0:t0 + 512], in_=ps_pj[:], func=AF.Identity, bias=kb_s[:, 0:1], scale=RS2),
                waits=[stt], inc=ss)

            def mmv(e):
                for dt_ in range(32):
                    i = e.matmul(ps_pj2[:], wvT[:, dt_ * 128:(dt_ + 1) * 128],
                                 xT[:, dt_ * 512:(dt_ + 1) * 512], start=(dt_ == 0), stop=(dt_ == 31))
                return i
            kb.emit("tensor", mmv, waits=["dma"], inc=stt)
            kb.emit("scalar", lambda e, t0=t0: e.activation(
                out=v_sb[:, t0:t0 + 512], in_=ps_pj2[:], func=AF.Identity, bias=vb_s[:, 0:1]),
                waits=[stt], inc=ss)

            for og in range(4):
                for dt_ in range(32):
                    kb.dma("sync", wqT[:, dt_ * 512:(dt_ + 1) * 512],
                           wqbf.ap()[og * 512:(og + 1) * 512, dt_ * 128:(dt_ + 1) * 128],
                           waits=[*n_cast, stt], transpose=True)
                for osub in range(4):
                    ot = og * 4 + osub
                    ps = ps_pj if (osub % 2 == 0) else ps_pj2

                    def mmq(e, osub=osub, ps=ps):
                        for dt_ in range(32):
                            i = e.matmul(ps[:], wqT[:, dt_ * 512 + osub * 128: dt_ * 512 + (osub + 1) * 128],
                                         xT[:, dt_ * 512:(dt_ + 1) * 512], start=(dt_ == 0), stop=(dt_ == 31))
                        return i
                    kb.emit("tensor", mmq, waits=["dma", ss], inc=stt)
                    kb.emit("scalar", lambda e, ot=ot, t0=t0, ps=ps: e.activation(
                        out=q_sb[:, ot * 2048 + t0: ot * 2048 + t0 + 512], in_=ps[:],
                        func=AF.Identity, bias=qb_s[:, ot:ot + 1]), waits=[stt], inc=ss)

        for rnd in range(2):
            def tr8(e, rnd=rnd):
                for j in range(8):
                    i = e.transpose(ps_pt[:, j * 128:(j + 1) * 128],
                                    v_sb[:, (rnd * 8 + j) * 128:(rnd * 8 + j + 1) * 128], identb_s[:])
                return i
            kb.emit("tensor", tr8, waits=[ss, sv], inc=stt)
            kb.emit("vector", lambda e, rnd=rnd: e.tensor_copy(vkd[:, rnd * 1024:(rnd + 1) * 1024], ps_pt[:]),
                    waits=[stt], inc=sv)

        for qt in range(16):
            kmax = (qt + 1) * 128
            nch = (kmax + 511) // 512
            for h_ in range(16):
                def mms(e, qt=qt, h_=h_, kmax=kmax, nch=nch):
                    for ch in range(nch):
                        c0 = ch * 512
                        c1 = min(kmax, c0 + 512)
                        i = e.matmul(ps_s[:, c0:c1],
                                     q_sb[:, h_ * 2048 + qt * 128: h_ * 2048 + (qt + 1) * 128],
                                     k_sb[:, c0:c1], start=True, stop=True)
                    return i
                kb.emit("tensor", mms, waits=[ss, sv], inc=stt)
                kb.emit("vector", lambda e, qt=qt, kmax=kmax: e.tensor_tensor(
                    out=ps_s[:, qt * 128:kmax], in0=ps_s[:, qt * 128:kmax], in1=mask_s[:], op=ALU.add),
                    waits=[stt], inc=sv)
                kb.emit("vector", lambda e, kmax=kmax: e.reduce_max(
                    nmax[:], ps_s[:, 0:kmax], axis=AX, negate=True), inc=sv)
                kb.emit("scalar", lambda e, kmax=kmax: e.activation(
                    out=p_sb[:, 0:kmax], in_=ps_s[:, 0:kmax], func=AF.Exp,
                    bias=nmax[:, 0:1], accum_out=lrow[:]), waits=[sv, stt], inc=ss)
                kb.emit("vector", lambda e: e.reciprocal(rl[:], lrow[:]), waits=[ss], inc=sv)
                kb.emit("vector", lambda e, kmax=kmax: e.tensor_scalar(
                    out=p_sb[:, 0:kmax], in0=p_sb[:, 0:kmax], scalar1=rl[:, 0:1], scalar2=None,
                    op0=ALU.mult), inc=sv)
                for g0 in range(0, qt + 1, 8):
                    g1 = min(qt + 1, g0 + 8)

                    def trp(e, g0=g0, g1=g1):
                        for j in range(g0, g1):
                            i = e.transpose(ps_pt[:, (j - g0) * 128:(j - g0 + 1) * 128],
                                            p_sb[:, j * 128:(j + 1) * 128], identb_s[:])
                        return i
                    kb.emit("tensor", trp, waits=[sv, ss], inc=stt)
                    kb.emit("scalar", lambda e, g0=g0, g1=g1: e.copy(
                        out=pt8[:, 0:(g1 - g0) * 128], in_=ps_pt[:, 0:(g1 - g0) * 128]),
                        waits=[stt], inc=ss)

                    def mav(e, g0=g0, g1=g1, qt=qt):
                        for j in range(g0, g1):
                            i = e.matmul(ps_att[:], vkd[:, j * 128:(j + 1) * 128],
                                         pt8[:, (j - g0) * 128:(j - g0 + 1) * 128],
                                         start=(j == 0), stop=(j == qt))
                        return i
                    kb.emit("tensor", mav, waits=[ss], inc=stt)
                kb.emit("scalar", lambda e: e.copy(out=att_sb[:], in_=ps_att[:]), waits=[stt, "dma"], inc=ss)
                kb.dma("sync", attn_o[h_ * 128:(h_ + 1) * 128, qt * 128:(qt + 1) * 128], att_sb[:],
                       waits=[ss])

        kb.emit("sync", lambda e: e.nop(), waits=["dma"])
        kb.build(es)
    return nc


def build_wo():
    nc = bass.Bass("TRN2", target_bir_lowering=False, debug=False)
    attn = nc.dram_tensor("attn", [D, TME], BF16, kind="ExternalInput").ap()
    wo = nc.dram_tensor("wo", [D, D], F32, kind="ExternalInput").ap()
    wob = nc.dram_tensor("wob", [128, 32], F32, kind="ExternalInput").ap()
    xmy = nc.dram_tensor("xmy", [TME, D], F32, kind="ExternalInput").ap()
    x1_o = nc.dram_tensor("x1", [D, TME], F32, kind="ExternalOutput").ap()
    wobf = nc.dram_tensor("wobf", [D, D], BF16)
    z_d = nc.dram_tensor("z_d", [D, TME], F32)
    cs = _consts(nc)

    kb = KB(nc)
    sv, ss, stt = (kb.sem(x) for x in ("sv", "ss", "stt"))
    kb.set_self("vector", sv)
    kb.set_self("scalar", ss)

    es = ExitStack()
    with es:
        E = lambda n, sh, dt: es.enter_context(nc.sbuf_tensor(n, sh, dt))
        a_sb = E("a_sb", [128, 32 * TME], BF16)
        woT = E("woT", [128, 32 * 512], BF16)
        xr_sb = E("xr_sb", [128, 8 * 128], F32)
        xres = E("xres", [128, TME], F32)
        ztl = E("ztl", [128, TME], F32)
        sq = E("sq", [128, TME], F32)
        bias_s = E("bias_s", [128, 32], F32)
        onescol_s = E("onescol_s", [128, 1], F32)
        ones1_s = E("ones1_s", [1, 128], F32)
        identf_s = E("identf_s", [128, 128], F32)
        mu = E("mu", [1, TME], F32)
        var = E("var", [1, TME], F32)
        rstd = E("rstd", [1, TME], F32)
        murep = E("murep", [128, TME], F32)
        rsrep = E("rsrep", [128, TME], F32)
        eps_s = E("eps_s", [128, 1], F32)

        ps_o = es.enter_context(nc.psum_tensor("ps_o", [128, 512], F32))
        ps_o2 = es.enter_context(nc.psum_tensor("ps_o2", [128, 512], F32))
        ps_xr = es.enter_context(nc.psum_tensor("ps_xr", [128, 1024], F32))
        ps_st = es.enter_context(nc.psum_tensor("ps_st", [1, 1024], F32))
        ps_st2 = es.enter_context(nc.psum_tensor("ps_st2", [1, 1024], F32))
        ps_rep = ps_xr

        for i in range(8):
            kb.dma("gpsimd", wobf.ap()[i * 512:(i + 1) * 512, :], wo[i * 512:(i + 1) * 512, :])
        n_cast = kb.mark()
        for dt_ in range(32):
            kb.dma("sync", a_sb[:, dt_ * TME:(dt_ + 1) * TME], attn[dt_ * 128:(dt_ + 1) * 128, :])
        kb.dma("sync", bias_s[:], wob)
        kb.dma("sync", onescol_s[:], cs["onescol"])
        kb.dma("sync", ones1_s[:], cs["ones1"])
        kb.dma("sync", identf_s[:], cs["identf"])
        kb.dma("sync", eps_s[:], cs["eps"])
        n_base = kb.mark()

        for og in range(8):
            for dt_ in range(32):
                kb.dma("sync", woT[:, dt_ * 512:(dt_ + 1) * 512],
                       wobf.ap()[og * 512:(og + 1) * 512, dt_ * 128:(dt_ + 1) * 128],
                       waits=[*n_cast, stt], transpose=True)
            for osub in range(4):
                ot = og * 4 + osub
                for tt in range(8):
                    kb.dma("scalar", xr_sb[:, tt * 128:(tt + 1) * 128],
                           xmy[tt * 128:(tt + 1) * 128, ot * 128:(ot + 1) * 128],
                           waits=[stt])

                def trx(e):
                    for tt in range(8):
                        i = e.matmul(ps_xr[:, tt * 128:(tt + 1) * 128], xr_sb[:, tt * 128:(tt + 1) * 128],
                                     identf_s[:], start=True, stop=True, is_transpose=True)
                    return i
                kb.emit("tensor", trx, waits=["dma", sv], inc=stt)
                kb.emit("vector", lambda e: e.tensor_copy(xres[:], ps_xr[:]), waits=[stt], inc=sv)

                def mmo(e, osub=osub):
                    for tc in range(2):
                        pst = ps_o if tc == 0 else ps_o2
                        for dt_ in range(32):
                            i = e.matmul(pst[:], woT[:, dt_ * 512 + osub * 128: dt_ * 512 + (osub + 1) * 128],
                                         a_sb[:, dt_ * TME + tc * 512: dt_ * TME + (tc + 1) * 512],
                                         start=(dt_ == 0), stop=(dt_ == 31))
                    return i
                kb.emit("tensor", mmo, waits=["dma", sv], inc=stt)
                kb.emit("vector", lambda e, ot=ot: e.scalar_tensor_tensor(
                    out=ztl[:, 0:512], in0=ps_o[:], scalar=bias_s[:, ot:ot + 1],
                    in1=xres[:, 0:512], op0=ALU.add, op1=ALU.add), waits=[stt, "dma"], inc=sv)
                kb.emit("vector", lambda e, ot=ot: e.scalar_tensor_tensor(
                    out=ztl[:, 512:1024], in0=ps_o2[:], scalar=bias_s[:, ot:ot + 1],
                    in1=xres[:, 512:1024], op0=ALU.add, op1=ALU.add), inc=sv)
                kb.emit("scalar", lambda e: e.square(out=sq[:], in_=ztl[:]), waits=[sv], inc=ss)

                def mst(e, ot=ot):
                    for tc in range(2):
                        e.matmul(ps_st[:, tc * 512:(tc + 1) * 512], onescol_s[:],
                                 ztl[:, tc * 512:(tc + 1) * 512], start=(ot == 0), stop=(ot == 31),
                                 skip_group_check=True)
                        i = e.matmul(ps_st2[:, tc * 512:(tc + 1) * 512], onescol_s[:],
                                     sq[:, tc * 512:(tc + 1) * 512], start=(ot == 0), stop=(ot == 31),
                                     skip_group_check=True)
                    return i
                kb.emit("tensor", mst, waits=[ss], inc=stt)
                kb.dma("sync", z_d.ap()[ot * 128:(ot + 1) * 128, :], ztl[:], waits=[ss])

        bufs = dict(sq=sq, mu=mu, var=var, rstd=rstd, murep=murep, rsrep=rsrep,
                    ones1_s=ones1_s, ps_st=ps_st, ps_st2=ps_st2,
                    ps_rep=ps_rep, ztl=ztl, eps_s=eps_s)
        kb.emit("vector", lambda e: e.engine_nop(), waits=["dma"], inc=sv)
        _ln_stream(kb, nc, TME, bufs, (sv, ss, stt), None, x1_o, z_d)
        kb.emit("sync", lambda e: e.nop(), waits=["dma"])
        kb.build(es)
    return nc


def build_dequant():
    nc = bass.Bass("TRN2", target_bir_lowering=False, debug=False)
    qw = nc.dram_tensor("qw", [512, 512], I32, kind="ExternalInput").ap()
    qz = nc.dram_tensor("qz", [32, 64], I32, kind="ExternalInput").ap()
    sc = nc.dram_tensor("sc", [32, 512], F32, kind="ExternalInput").ap()
    gidx = nc.dram_tensor("gidx", [D, 1], I32, kind="ExternalInput").ap()
    w_o = nc.dram_tensor("wslice", [D, 512], BF16, kind="ExternalOutput").ap()
    shifts_c = nc.inline_tensor((4 * (np.arange(128, dtype=np.int32) % 8)).reshape(128, 1),
                                name="c_shifts").ap()
    iota32_c = nc.inline_tensor(np.broadcast_to(np.arange(32, dtype=np.float32), (128, 32)).copy(),
                                name="c_iota32").ap()
    cs = _consts(nc)

    kb = KB(nc)
    sv, ss, stt = (kb.sem(x) for x in ("sv", "ss", "stt"))
    kb.set_self("vector", sv)
    kb.set_self("scalar", ss)

    es = ExitStack()
    with es:
        E = lambda n, sh, dt: es.enter_context(nc.sbuf_tensor(n, sh, dt))
        qz_s = E("qz_s", [32, 64], I32)
        zi = E("zi", [32, 512], I32)
        zf = E("zf", [32, 512], F32)
        zs = E("zs", [32, 512], F32)
        sc_s = E("sc_s", [32, 512], F32)
        gix = E("gix", [128, 1], I32)
        gixf = E("gixf", [128, 1], F32)
        oh = E("oh", [128, 32], F32)
        ohT = E("ohT", [32, 128], F32)
        iota_s = E("iota_s", [128, 32], F32)
        shf_s = E("shf_s", [128, 1], I32)
        qrep = E("qrep", [128, 512], I32)
        qi = E("qi", [128, 512], I32)
        qf = E("qf", [128, 512], F32)
        wtmp = E("wtmp", [128, 512], F32)
        wout = E("wout", [128, 512], BF16)
        identf_s = E("identf_s", [128, 128], F32)

        ps_oh = es.enter_context(nc.psum_tensor("ps_oh", [32, 128], F32))
        ps_s1 = es.enter_context(nc.psum_tensor("ps_s1", [128, 512], F32))
        ps_s2 = es.enter_context(nc.psum_tensor("ps_s2", [128, 512], F32))

        kb.dma("sync", qz_s[:], qz)
        kb.dma("sync", sc_s[:], sc)
        kb.dma("sync", shf_s[:], shifts_c)
        kb.dma("sync", iota_s[:], iota32_c)
        kb.dma("sync", identf_s[:], cs["identf"])
        n0 = kb.mark()

        for s_ in range(8):
            kb.emit("vector", lambda e, s_=s_: e.tensor_scalar(
                out=zi[:, :].rearrange("p (c e) -> p c e", e=8)[:, :, s_], in0=qz_s[:],
                scalar1=4 * s_, scalar2=15, op0=ALU.logical_shift_right, op1=ALU.bitwise_and),
                waits=n0, inc=sv)
        kb.emit("vector", lambda e: e.tensor_scalar_add(zi[:], zi[:], 1), inc=sv)
        kb.emit("vector", lambda e: e.tensor_copy(zf[:], zi[:]), inc=sv)
        kb.emit("vector", lambda e: e.tensor_tensor(out=zs[:], in0=zf[:], in1=sc_s[:], op=ALU.mult), inc=sv)

        for ki in range(32):
            kb.dma("sync", gix[:], gidx[ki * 128:(ki + 1) * 128, :], waits=[sv])
            kb.dma("gpsimd", qrep[:], qw[ki * 16:(ki + 1) * 16, :].unsqueeze(1).broadcast_to([16, 8, 512]),
                   waits=[sv])
            kb.emit("vector", lambda e: e.tensor_copy(gixf[:], gix[:]), waits=["dma", stt], inc=sv)
            kb.emit("vector", lambda e: e.tensor_scalar(
                out=oh[:], in0=iota_s[:], scalar1=gixf[:, 0:1], scalar2=None, op0=ALU.is_equal), inc=sv)
            kb.emit("tensor", lambda e: e.transpose(ps_oh[:], oh[:], identf_s[:]), waits=[sv, ss], inc=stt)
            kb.emit("scalar", lambda e: e.copy(out=ohT[:], in_=ps_oh[:]), waits=[stt], inc=ss)

            def gsc(e):
                e.matmul(ps_s1[:], ohT[:], sc_s[:], start=True, stop=True)
                return e.matmul(ps_s2[:], ohT[:], zs[:], start=True, stop=True)
            kb.emit("tensor", gsc, waits=[ss, sv], inc=stt)

            kb.emit("vector", lambda e: e.tensor_scalar(
                out=qi[:], in0=qrep[:], scalar1=shf_s[:, 0:1], scalar2=15,
                op0=ALU.logical_shift_right, op1=ALU.bitwise_and), waits=["dma", stt], inc=sv)
            kb.emit("vector", lambda e: e.tensor_copy(qf[:], qi[:]), inc=sv)
            kb.emit("vector", lambda e: e.tensor_tensor(out=wtmp[:], in0=qf[:], in1=ps_s1[:], op=ALU.mult),
                    inc=sv)
            kb.emit("vector", lambda e: e.tensor_tensor(out=wout[:], in0=wtmp[:], in1=ps_s2[:], op=ALU.subtract),
                    waits=["dma"], inc=sv)
            kb.dma("sync", w_o[ki * 128:(ki + 1) * 128, :], wout[:], waits=[sv])

        kb.emit("sync", lambda e: e.nop(), waits=["dma"])
        kb.build(es)
    return nc


def build_ffn():
    nc = bass.Bass("TRN2", target_bir_lowering=False, debug=False)
    x1 = nc.dram_tensor("x1", [D, TME], F32, kind="ExternalInput").ap()
    wfull = nc.dram_tensor("wfull", [D, D], BF16, kind="ExternalInput").ap()
    fbias = nc.dram_tensor("fbias", [128, 32], F32, kind="ExternalInput").ap()
    x2_o = nc.dram_tensor("x2", [D, TME], F32, kind="ExternalOutput").ap()
    z_d = nc.dram_tensor("z_d", [D, TME], F32)
    cs = _consts(nc)

    kb = KB(nc)
    sv, ss, stt = (kb.sem(x) for x in ("sv", "ss", "stt"))
    kb.set_self("vector", sv)
    kb.set_self("scalar", ss)

    es = ExitStack()
    with es:
        E = lambda n, sh, dt: es.enter_context(nc.sbuf_tensor(n, sh, dt))
        x1b = E("x1b", [128, 32 * TME], BF16)
        wblk = E("wblk", [128, 32 * 512], BF16)
        x1t = E("x1t", [128, TME], F32)
        ztl = E("ztl", [128, TME], F32)
        sq = E("sq", [128, TME], F32)
        bias_s = E("bias_s", [128, 32], F32)
        onescol_s = E("onescol_s", [128, 1], F32)
        ones1_s = E("ones1_s", [1, 128], F32)
        mu = E("mu", [1, TME], F32)
        var = E("var", [1, TME], F32)
        rstd = E("rstd", [1, TME], F32)
        murep = E("murep", [128, TME], F32)
        rsrep = E("rsrep", [128, TME], F32)
        eps_s = E("eps_s", [128, 1], F32)

        ps_o = es.enter_context(nc.psum_tensor("ps_o", [128, 512], F32))
        ps_o2 = es.enter_context(nc.psum_tensor("ps_o2", [128, 512], F32))
        ps_st = es.enter_context(nc.psum_tensor("ps_st", [1, 1024], F32))
        ps_st2 = es.enter_context(nc.psum_tensor("ps_st2", [1, 1024], F32))
        ps_rep = es.enter_context(nc.psum_tensor("ps_rep", [128, 1024], F32))

        kb.dma("sync", bias_s[:], fbias)
        kb.dma("sync", onescol_s[:], cs["onescol"])
        kb.dma("sync", ones1_s[:], cs["ones1"])
        kb.dma("sync", eps_s[:], cs["eps"])
        for dt_ in range(32):
            kb.dma("sync", x1t[:], x1[dt_ * 128:(dt_ + 1) * 128, :], waits=[sv])
            kb.emit("vector", lambda e, dt_=dt_: e.tensor_copy(
                x1b[:, dt_ * TME:(dt_ + 1) * TME], x1t[:]), waits=["dma"], inc=sv)
        n_x1b = sv.n

        for og in range(8):
            for dt_ in range(32):
                kb.dma("sync", wblk[:, dt_ * 512:(dt_ + 1) * 512],
                       wfull[dt_ * 128:(dt_ + 1) * 128, og * 512:(og + 1) * 512], waits=[stt])
            for osub in range(4):
                ot = og * 4 + osub
                kb.dma("scalar", x1t[:], x1[ot * 128:(ot + 1) * 128, :], waits=[sv])

                def mmf(e, osub=osub):
                    for tc in range(2):
                        pst = ps_o if tc == 0 else ps_o2
                        for dt_ in range(32):
                            i = e.matmul(pst[:], wblk[:, dt_ * 512 + osub * 128: dt_ * 512 + (osub + 1) * 128],
                                         x1b[:, dt_ * TME + tc * 512: dt_ * TME + (tc + 1) * 512],
                                         start=(dt_ == 0), stop=(dt_ == 31))
                    return i
                kb.emit("tensor", mmf, waits=[(dq_sync, n_wload[wp]), (sv, n_x1b), (sv, n_evac[pp])],
                    inc=stt)
                kb.emit("vector", lambda e, ot=ot: e.scalar_tensor_tensor(
                    out=ztl[:, 0:512], in0=ps_o[:], scalar=bias_s[:, ot:ot + 1],
                    in1=x1t[:, 0:512], op0=ALU.add, op1=ALU.add), waits=[stt, "dma"], inc=sv)
                kb.emit("vector", lambda e, ot=ot: e.scalar_tensor_tensor(
                    out=ztl[:, 512:1024], in0=ps_o2[:], scalar=bias_s[:, ot:ot + 1],
                    in1=x1t[:, 512:1024], op0=ALU.add, op1=ALU.add), inc=sv)
                kb.emit("scalar", lambda e: e.square(out=sq[:], in_=ztl[:]), waits=[sv], inc=ss)

                def mst(e, ot=ot):
                    for tc in range(2):
                        e.matmul(ps_st[:, tc * 512:(tc + 1) * 512], onescol_s[:],
                                 ztl[:, tc * 512:(tc + 1) * 512], start=(ot == 0), stop=(ot == 31),
                                 skip_group_check=True)
                        i = e.matmul(ps_st2[:, tc * 512:(tc + 1) * 512], onescol_s[:],
                                     sq[:, tc * 512:(tc + 1) * 512], start=(ot == 0), stop=(ot == 31),
                                     skip_group_check=True)
                    return i
                kb.emit("tensor", mst, waits=[ss], inc=stt)
                kb.dma("sync", z_d.ap()[ot * 128:(ot + 1) * 128, :], ztl[:], waits=[ss])

        bufs = dict(sq=sq, mu=mu, var=var, rstd=rstd, murep=murep, rsrep=rsrep,
                    ones1_s=ones1_s, ps_st=ps_st, ps_st2=ps_st2,
                    ps_rep=ps_rep, ztl=ztl, eps_s=eps_s)
        kb.emit("vector", lambda e: e.engine_nop(), waits=["dma"], inc=sv)
        _ln_stream(kb, nc, TME, bufs, (sv, ss, stt), None, x2_o, z_d)
        kb.emit("sync", lambda e: e.nop(), waits=["dma"])
        kb.build(es)
    return nc


def build_ssm():
    nc = bass.Bass("TRN2", target_bir_lowering=False, debug=False)
    xr = nc.dram_tensor("xr", [2048, T], F32, kind="ExternalInput").ap()
    x2f = nc.dram_tensor("x2f", [D, T], F32, kind="ExternalInput").ap()
    xproj = nc.dram_tensor("xproj", [160, D], F32, kind="ExternalInput").ap()
    dtw = nc.dram_tensor("dtw", [2048, 128], F32, kind="ExternalInput").ap()
    dtb = nc.dram_tensor("dtb", [128, 16], F32, kind="ExternalInput").ap()
    alog = nc.dram_tensor("alog", [2048, 16], F32, kind="ExternalInput").ap()
    dvec = nc.dram_tensor("dvec", [128, 16], F32, kind="ExternalInput").ap()
    z_o = nc.dram_tensor("z", [2048, T], F32, kind="ExternalOutput").ap()

    x2bf = nc.dram_tensor("x2bf", [D, T], BF16)
    xpjbf = nc.dram_tensor("xpjbf", [160, D], BF16)
    dtwbf = nc.dram_tensor("dtwbf", [2048, 128], BF16)
    bc_d = nc.dram_tensor("bc_d", [32, T], F32)
    cs = _consts(nc)
    TC = 512

    kb = KB(nc)
    sv, ss, stt, sg = (kb.sem(x) for x in ("sv", "ss", "stt", "sg"))
    kb.set_self("vector", sv)
    kb.set_self("scalar", ss)
    kb.set_self("gpsimd", sg)

    es = ExitStack()
    with es:
        E = lambda n, sh, dt: es.enter_context(nc.sbuf_tensor(n, sh, dt))
        xpT = E("xpT", [128, 512], BF16)
        xpj1 = E("xpj1", [128, 32 * 128], BF16)
        xpj2 = E("xpj2", [128, 32 * 32], BF16)
        xp_sb = E("xp_sb", [128, T], F32)
        dtinb = E("dtinb", [128, T], BF16)
        bc_sb = E("bc_sb", [32, T], F32)
        brep = E("brep", [128, 16 * TC], F32)
        crep = E("crep", [128, 16 * TC], F32)
        dtw_s = E("dtw_s", [128, 16 * 128], BF16)
        dtb_s = E("dtb_s", [128, 16], F32)
        a_s = E("a_s", [128, 16 * 16], F32)
        d1_s = E("d1_s", [128, 16], F32)
        hcar = E("hcar", [128, 16 * 16], F32)
        dt_t = E("dt_t", [128, TC], F32)
        sp_a = E("sp_a", [128, TC], F32)
        sp_e = E("sp_e", [128, TC], F32)
        u_t = E("u_t", [128, TC], F32)
        xr_t = E("xr_t", [128, TC], F32)
        dA_t = E("dA_t", [128, TC], F32)
        dBx_t = E("dBx_t", [128, TC], F32)
        h_t = E("h_t", [128, TC], F32)
        ch_t = E("ch_t", [128, TC], F32)
        y_t = E("y_t", [128, TC], F32)
        z_t = E("z_t", [128, TC], F32)

        ps_xp = es.enter_context(nc.psum_tensor("ps_xp", [128, 512], F32))
        ps_xp2 = es.enter_context(nc.psum_tensor("ps_xp2", [32, 512], F32))
        ps_dt = es.enter_context(nc.psum_tensor("ps_dt", [128, 512], F32))

        for i in range(8):
            kb.dma("gpsimd", x2bf.ap()[i * 512:(i + 1) * 512, :], x2f[i * 512:(i + 1) * 512, :])
        kb.dma("gpsimd", xpjbf.ap(), xproj)
        kb.dma("gpsimd", dtwbf.ap(), dtw)
        n_cast = kb.mark()

        kb.dma("sync", dtb_s[:], dtb)
        kb.dma("sync", a_s[:], alog.rearrange("(k p) n -> p k n", p=128))
        kb.dma("sync", d1_s[:], dvec)
        for kt in range(16):
            kb.dma("sync", dtw_s[:, kt * 128:(kt + 1) * 128], dtwbf.ap()[kt * 128:(kt + 1) * 128, 0:128],
                   waits=n_cast, transpose=True)
        for dt_ in range(32):
            kb.dma("sync", xpj1[:, dt_ * 128:(dt_ + 1) * 128], xpjbf.ap()[0:128, dt_ * 128:(dt_ + 1) * 128],
                   waits=n_cast, transpose=True)
            kb.dma("sync", xpj2[:, dt_ * 32:(dt_ + 1) * 32], xpjbf.ap()[128:160, dt_ * 128:(dt_ + 1) * 128],
                   waits=n_cast, transpose=True)
        n_small = kb.mark()
        kb.emit("scalar", lambda e: e.activation(out=a_s[:], in_=a_s[:], func=AF.Exp),
                waits=n_small, inc=ss)
        kb.emit("vector", lambda e: e.tensor_scalar_mul(a_s[:], a_s[:], -1.0), waits=[ss], inc=sv)
        kb.emit("vector", lambda e: e.tensor_scalar_add(d1_s[:], d1_s[:], 1.0), inc=sv)
        kb.emit("vector", lambda e: e.memset(hcar[:], 0.0), inc=sv)

        for tch in range(4):
            t0 = tch * 512
            for dt_ in range(32):
                kb.dma("sync", xpT[:], x2bf.ap()[dt_ * 128:(dt_ + 1) * 128, t0:t0 + 512],
                       waits=[*n_cast, stt])

                def mmxp(e, dt_=dt_):
                    e.matmul(ps_xp[:], xpj1[:, dt_ * 128:(dt_ + 1) * 128], xpT[:],
                             start=(dt_ == 0), stop=(dt_ == 31))
                    return e.matmul(ps_xp2[:], xpj2[:, dt_ * 32:(dt_ + 1) * 32], xpT[:],
                                    start=(dt_ == 0), stop=(dt_ == 31))
                kb.emit("tensor", mmxp, waits=["dma", ss], inc=stt)
            kb.emit("scalar", lambda e, t0=t0: e.copy(out=xp_sb[:, t0:t0 + 512], in_=ps_xp[:]),
                    waits=[stt], inc=ss)
            kb.emit("scalar", lambda e, t0=t0: e.copy(out=bc_sb[:, t0:t0 + 512], in_=ps_xp2[:]), inc=ss)
        for tch in range(4):
            kb.emit("vector", lambda e, tch=tch: e.tensor_copy(
                dtinb[:, tch * 512:(tch + 1) * 512], xp_sb[:, tch * 512:(tch + 1) * 512]), waits=[ss], inc=sv)
        kb.dma("sync", bc_d.ap(), bc_sb[:], waits=[ss])
        n_bc = kb.mark()

        NCH = T // TC
        for tc_i in range(NCH):
            tt0 = tc_i * TC
            for n_ in range(16):
                kb.dma("gpsimd", brep[:, n_ * TC:(n_ + 1) * TC],
                       bc_d.ap()[n_:n_ + 1, tt0:tt0 + TC].broadcast_to([128, TC]),
                       waits=[*n_bc, sg, sv])
                kb.dma("gpsimd", crep[:, n_ * TC:(n_ + 1) * TC],
                       bc_d.ap()[16 + n_:17 + n_, tt0:tt0 + TC].broadcast_to([128, TC]),
                       waits=n_bc)
            n_rep = kb.mark()

            for kt in range(16):
                kb.emit("tensor", lambda e, kt=kt, tt0=tt0: e.matmul(
                    ps_dt[:], dtw_s[:, kt * 128:(kt + 1) * 128],
                    dtinb[:, tt0:tt0 + TC], start=True, stop=True),
                    waits=[sv, ss, *n_small], inc=stt)
                kb.emit("scalar", lambda e, kt=kt: e.activation(
                    out=dt_t[:], in_=ps_dt[:], func=AF.Identity,
                    bias=dtb_s[:, kt:kt + 1]), waits=[stt], inc=ss)
                kb.emit("scalar", lambda e: e.activation(out=sp_a[:], in_=dt_t[:], func=AF.Abs), inc=ss)
                kb.emit("scalar", lambda e: e.activation(out=sp_e[:], in_=sp_a[:], func=AF.Exp, scale=-1.0),
                        inc=ss)
                kb.emit("scalar", lambda e: e.activation(out=sp_e[:], in_=sp_e[:], func=AF.Ln, bias=1.0), inc=ss)
                kb.emit("vector", lambda e: e.tensor_scalar_max(dt_t[:], dt_t[:], 0.0), waits=[ss], inc=sv)
                kb.emit("vector", lambda e: e.tensor_tensor(out=dt_t[:], in0=dt_t[:], in1=sp_e[:], op=ALU.add),
                        inc=sv)
                kb.dma("sync", xr_t[:], xr[kt * 128:(kt + 1) * 128, tt0:tt0 + TC], waits=[sv, sg])
                kb.emit("vector", lambda e: e.tensor_tensor(out=u_t[:], in0=dt_t[:], in1=xr_t[:], op=ALU.mult),
                        waits=["dma"], inc=sv)
                kb.emit("gpsimd", lambda e: e.memset(y_t[:], 0.0), waits=[sv], inc=sg)

                for n_ in range(16):
                    kb.emit("scalar", lambda e, kt=kt, n_=n_: e.activation(
                        out=dA_t[:], in_=dt_t[:], func=AF.Exp,
                        scale=a_s[:, kt * 16 + n_: kt * 16 + n_ + 1]), waits=[sv, sg], inc=ss)
                    kb.emit("gpsimd", lambda e, n_=n_: e.tensor_tensor(
                        out=dBx_t[:], in0=u_t[:], in1=brep[:, n_ * TC:(n_ + 1) * TC], op=ALU.mult),
                        waits=[*n_rep, sv], inc=sg)
                    if tc_i == 0:
                        kb.emit("vector", lambda e: e.tensor_tensor_scan(
                            out=h_t[:], data0=dA_t[:], data1=dBx_t[:], initial=0.0,
                            op0=ALU.mult, op1=ALU.add), waits=[ss, sg], inc=sv)
                    else:
                        kb.emit("vector", lambda e, kt=kt, n_=n_: e.tensor_tensor_scan(
                            out=h_t[:], data0=dA_t[:], data1=dBx_t[:],
                            initial=hcar[:, kt * 16 + n_: kt * 16 + n_ + 1],
                            op0=ALU.mult, op1=ALU.add), waits=[ss, sg], inc=sv)
                    if tc_i < NCH - 1:
                        kb.emit("vector", lambda e, kt=kt, n_=n_: e.tensor_copy(
                            hcar[:, kt * 16 + n_: kt * 16 + n_ + 1], h_t[:, TC - 1:TC]), inc=sv)
                    kb.emit("gpsimd", lambda e, n_=n_: e.tensor_tensor(
                        out=ch_t[:], in0=h_t[:], in1=crep[:, n_ * TC:(n_ + 1) * TC], op=ALU.mult),
                        waits=[sv], inc=sg)
                    kb.emit("vector", lambda e: e.tensor_tensor(out=y_t[:], in0=y_t[:], in1=ch_t[:],
                                                                op=ALU.add), waits=[sg], inc=sv)
                kb.emit("vector", lambda e, kt=kt: e.scalar_tensor_tensor(
                    out=z_t[:], in0=xr_t[:], scalar=d1_s[:, kt:kt + 1], in1=y_t[:],
                    op0=ALU.mult, op1=ALU.add), waits=["dma"], inc=sv)
                kb.dma("sync", z_o[kt * 128:(kt + 1) * 128, tt0:tt0 + TC], z_t[:], waits=[sv])

        kb.emit("sync", lambda e: e.nop(), waits=["dma"])
        kb.build(es)
    return nc


def build_final():
    nc = bass.Bass("TRN2", target_bir_lowering=False, debug=False)
    zall = nc.dram_tensor("zall", [D, NCORES * TME], F32, kind="ExternalInput").ap()
    w1 = nc.dram_tensor("w1", [D, D], F32, kind="ExternalInput").ap()
    loraA = nc.dram_tensor("loraA", [8 * D, 16], F32, kind="ExternalInput").ap()
    loraB = nc.dram_tensor("loraB", [128, D], F32, kind="ExternalInput").ap()
    gw1 = nc.dram_tensor("gw1", [32, D], F32, kind="ExternalInput").ap()
    gb1 = nc.dram_tensor("gb1", [32, 1], F32, kind="ExternalInput").ap()
    lng = nc.dram_tensor("lng", [1, 32], F32, kind="ExternalInput").ap()
    lnb = nc.dram_tensor("lnb", [1, 32], F32, kind="ExternalInput").ap()
    gw2 = nc.dram_tensor("gw2", [32, 8], F32, kind="ExternalInput").ap()
    gb2 = nc.dram_tensor("gb2", [8, 1], F32, kind="ExternalInput").ap()
    out_o = nc.dram_tensor("out", [TME, D], F32, kind="ExternalOutput").ap()

    w1bf = nc.dram_tensor("w1bf", [D, D], BF16)
    gw1bf = nc.dram_tensor("gw1bf", [32, D], BF16)
    x3_d = nc.dram_tensor("x3_d", [D, TME], F32)
    allw_d = nc.dram_tensor("allw_d", [2, 8], F32)
    zf_d = nc.dram_tensor("zf_d", [TME, D], F32)
    cs = _consts(nc)
    repmat = np.zeros((8, 128), np.float32)
    for e_ in range(8):
        repmat[e_, e_ * 16:(e_ + 1) * 16] = 1.0 / 16.0
    rep_c = nc.inline_tensor(repmat, name="c_repmat").ap()
    ones8_c = nc.inline_tensor(np.ones((128, 8), np.float32), name="c_ones8").ap()

    kb = KB(nc)
    sv, ss, stt = (kb.sem(x) for x in ("sv", "ss", "stt"))
    kb.set_self("vector", sv)
    kb.set_self("scalar", ss)
    NT = NCORES * TME

    es = ExitStack()
    with es:
        E = lambda n, sh, dt: es.enter_context(nc.sbuf_tensor(n, sh, dt))
        zt = E("zt", [128, 512], F32)
        ztb = E("ztb", [128, 512], BF16)
        sqs = E("sqs", [128, 512], F32)
        x3b = E("x3b", [128, 32 * TME], BF16)
        gw1T = E("gw1T", [128, 32 * 32], BF16)
        gw2T_s = E("gw2T_s", [32, 8], BF16)
        hf = E("hf", [32, 512], F32)
        hfb = E("hfb", [32, 512], BF16)
        gf = E("gf", [8, 512], F32)
        ht = E("ht", [128, 4 * 32], F32)
        gt = E("gt", [128, 8], F32)
        stats = E("stats", [1, 512], F32)
        stats2 = E("stats2", [1, 512], F32)
        mu_r = E("mu_r", [128, 512], F32)
        rs_r = E("rs_r", [128, 512], F32)
        onescol_s = E("onescol_s", [128, 1], F32)
        ones1_s = E("ones1_s", [1, 128], F32)
        identf_s = E("identf_s", [128, 128], F32)
        gb1_s = E("gb1_s", [32, 1], F32)
        gb2_s = E("gb2_s", [8, 1], F32)
        lng_r = E("lng_r", [128, 32], F32)
        lnb_r = E("lnb_r", [128, 32], F32)
        gmu = E("gmu", [128, 1], F32)
        gvar = E("gvar", [128, 1], F32)
        grs = E("grs", [128, 1], F32)
        hsq = E("hsq", [128, 32], F32)
        tm8 = E("tm8", [128, 8], F32)
        oh1 = E("oh1", [128, 8], F32)
        oh2 = E("oh2", [128, 8], F32)
        cum = E("cum", [128, 8], F32)
        ones8_s = E("ones8_s", [128, 8], F32)
        vmax = E("vmax", [128, 1], F32)
        v2max = E("v2max", [128, 1], F32)
        eg = E("eg", [128, 1], F32)
        g1c = E("g1c", [128, 1], F32)
        g2c = E("g2c", [128, 1], F32)
        oh1my = E("oh1my", [128, 8 * 8], F32)
        oh2my = E("oh2my", [128, 8 * 8], F32)
        g1my = E("g1my", [128, 8], F32)
        g2my = E("g2my", [128, 8], F32)
        cnt = E("cnt", [8, 2], F32)
        allw = E("allw", [8, 2], F32)
        allwT = E("allwT", [2, 16], F32)
        al0_r = E("al0_r", [128, 8], F32)
        al1_r = E("al1_r", [128, 8], F32)
        wtk = E("wtk", [128, 8], F32)
        wtk2 = E("wtk2", [128, 8], F32)
        wT_sb = E("wT_sb", [8, TME], F32)
        wrep = E("wrep", [128, TME], F32)
        repm_s = E("repm_s", [8, 128], F32)
        loraA_s = E("loraA_s", [128, 32 * 128], BF16)
        loraB_s = E("loraB_s", [128, D], BF16)
        tmpm = E("tmpm", [128, 2 * 512], BF16)
        w1T = E("w1T", [128, 32 * 512], BF16)
        x3res = E("x3res", [128, 512], F32)
        zft = E("zft", [128, 512], F32)
        fstat = E("fstat", [1, TME], F32)
        fstat2 = E("fstat2", [1, TME], F32)
        fmuT = E("fmuT", [128, 8], F32)
        frsT = E("frsT", [128, 8], F32)
        eps_s = E("eps_s", [128, 1], F32)
        outt = E("outt", [128, D], F32)

        ps_a = es.enter_context(nc.psum_tensor("ps_a", [128, 512], F32))
        ps_h = es.enter_context(nc.psum_tensor("ps_h", [32, 512], F32))
        ps_t = es.enter_context(nc.psum_tensor("ps_t", [128, 512], F32))
        ps_st = es.enter_context(nc.psum_tensor("ps_st", [1, 1024], F32))
        ps_st2 = es.enter_context(nc.psum_tensor("ps_st2", [1, 1024], F32))
        ps_cnt = es.enter_context(nc.psum_tensor("ps_cnt", [8, 2], F32))
        ps_g = ps_h

        for i in range(8):
            kb.dma("gpsimd", w1bf.ap()[i * 512:(i + 1) * 512, :], w1[i * 512:(i + 1) * 512, :])
        for dt_ in range(32):
            kb.dma("gpsimd", loraA_s[:, dt_ * 128:(dt_ + 1) * 128].rearrange("p (e r) -> p e r", e=8),
                   loraA.rearrange("(e d) r -> d e r", e=8)[dt_ * 128:(dt_ + 1) * 128, :, :])
        kb.dma("gpsimd", loraB_s[:], loraB)
        kb.dma("gpsimd", gw1bf.ap(), gw1)
        kb.dma("gpsimd", gw2T_s[:], gw2)
        kb.dma("gpsimd", lng_r[:], lng.broadcast_to([128, 32]))
        kb.dma("gpsimd", lnb_r[:], lnb.broadcast_to([128, 32]))
        n_cast = kb.mark()
        kb.dma("sync", onescol_s[:], cs["onescol"])
        kb.dma("sync", ones1_s[:], cs["ones1"])
        kb.dma("sync", identf_s[:], cs["identf"])
        kb.dma("sync", gb1_s[:], gb1)
        kb.dma("sync", gb2_s[:], gb2)
        kb.dma("sync", ones8_s[:], ones8_c)
        kb.dma("sync", repm_s[:], rep_c)
        for dt_ in range(32):
            kb.dma("sync", gw1T[:, dt_ * 32:(dt_ + 1) * 32],
                   gw1bf.ap()[:, dt_ * 128:(dt_ + 1) * 128],
                   waits=n_cast, transpose=True)
        n_small = kb.mark()

        for tch in range(NT // 512):
            mine = tch < 2
            for dt_ in range(32):
                kb.dma("sync", zt[:], zall[dt_ * 128:(dt_ + 1) * 128, tch * 512:(tch + 1) * 512],
                       waits=[ss, stt])
                kb.emit("scalar", lambda e: e.square(out=sqs[:], in_=zt[:]), waits=["dma", stt], inc=ss)

                def mst(e, dt_=dt_):
                    e.matmul(ps_st[:, 0:512], onescol_s[:], zt[:], start=(dt_ == 0), stop=(dt_ == 31),
                             skip_group_check=True)
                    return e.matmul(ps_st2[:, 0:512], onescol_s[:], sqs[:], start=(dt_ == 0),
                                    stop=(dt_ == 31), skip_group_check=True)
                kb.emit("tensor", mst, waits=[ss, *n_small], inc=stt)
            kb.emit("vector", lambda e: e.tensor_scalar_mul(stats[:], ps_st[:, 0:512], 1.0 / D),
                    waits=[stt], inc=sv)
            kb.emit("vector", lambda e: e.tensor_scalar_mul(stats2[:], ps_st2[:, 0:512], 1.0 / D), inc=sv)
            kb.emit("vector", lambda e: e.tensor_tensor(out=sqs[0:1, :], in0=stats[:], in1=stats[:],
                                                        op=ALU.mult), inc=sv)
            kb.emit("vector", lambda e: e.tensor_tensor(out=stats2[:], in0=stats2[:], in1=sqs[0:1, :],
                                                        op=ALU.subtract), inc=sv)
            kb.emit("scalar", lambda e: e.activation(out=stats2[:], in_=stats2[:], func=AF.Ln,
                                                     bias=eps_s[0:1, 0:1]), waits=[sv], inc=ss)
            kb.emit("scalar", lambda e: e.activation(out=stats2[:], in_=stats2[:], func=AF.Exp, scale=-0.5),
                    inc=ss)
            kb.emit("vector", lambda e: e.tensor_scalar_mul(stats[:], stats[:], -1.0), waits=[ss], inc=sv)

            kb.emit("tensor", lambda e: e.matmul(ps_a[:], ones1_s[:], stats[0:1, :],
                                                 start=True, stop=True), waits=[sv, ss], inc=stt)
            kb.emit("scalar", lambda e: e.copy(out=mu_r[:], in_=ps_a[:]), waits=[stt], inc=ss)
            kb.emit("tensor", lambda e: e.matmul(ps_a[:], ones1_s[:], stats2[0:1, :],
                                                 start=True, stop=True), waits=[ss], inc=stt)
            kb.emit("scalar", lambda e: e.copy(out=rs_r[:], in_=ps_a[:]), waits=[stt], inc=ss)

            for dt_ in range(32):
                kb.dma("sync", zt[:], zall[dt_ * 128:(dt_ + 1) * 128, tch * 512:(tch + 1) * 512],
                       waits=[ss, stt, sv])
                kb.emit("vector", lambda e: e.tensor_tensor(out=zt[:], in0=zt[:], in1=mu_r[:], op=ALU.add),
                        waits=["dma", ss], inc=sv)
                kb.emit("vector", lambda e: e.tensor_tensor(out=zt[:], in0=zt[:], in1=rs_r[:], op=ALU.mult),
                        inc=sv)
                if mine:
                    kb.emit("vector", lambda e, dt_=dt_, tch=tch: e.tensor_copy(
                        x3b[:, dt_ * TME + tch * 512: dt_ * TME + (tch + 1) * 512], zt[:]), inc=sv)
                    kb.dma("scalar", x3_d.ap()[dt_ * 128:(dt_ + 1) * 128, tch * 512:(tch + 1) * 512], zt[:],
                           waits=[sv])
                kb.emit("vector", lambda e: e.tensor_copy(ztb[:], zt[:]), inc=sv)

                def mh(e, dt_=dt_):
                    return e.matmul(ps_h[:], gw1T[:, dt_ * 32:(dt_ + 1) * 32], ztb[:],
                                    start=(dt_ == 0), stop=(dt_ == 31))
                kb.emit("tensor", mh, waits=[sv], inc=stt)

            kb.emit("scalar", lambda e: e.activation(out=hf[:], in_=ps_h[:], func=AF.Identity,
                                                     bias=gb1_s[:, 0:1]), waits=[stt], inc=ss)
            def trh(e):
                for tt in range(4):
                    i = e.transpose(ps_t[:, tt * 32:(tt + 1) * 32], hf[0:32, tt * 128:(tt + 1) * 128],
                                    identf_s[0:32, 0:32])
                return i
            kb.emit("tensor", trh, waits=[ss, sv], inc=stt)
            kb.emit("vector", lambda e: e.tensor_copy(ht[:], ps_t[:, 0:128]), waits=[stt], inc=sv)
            for tt in range(4):
                htt = ht[:, tt * 32:(tt + 1) * 32]
                kb.emit("vector", lambda e, htt=htt: e.reduce_sum(gmu[:], htt, axis=AX), inc=sv)
                kb.emit("vector", lambda e: e.tensor_scalar_mul(gmu[:], gmu[:], 1.0 / 32.0), inc=sv)
                kb.emit("vector", lambda e, htt=htt: e.tensor_scalar(
                    out=htt, in0=htt, scalar1=gmu[:, 0:1], scalar2=None, op0=ALU.subtract), inc=sv)
                kb.emit("scalar", lambda e, htt=htt: e.square(out=hsq[:], in_=htt), waits=[sv], inc=ss)
                kb.emit("vector", lambda e: e.reduce_sum(gvar[:], hsq[:], axis=AX), waits=[ss], inc=sv)
                kb.emit("vector", lambda e: e.tensor_scalar_mul(gvar[:], gvar[:], 1.0 / 32.0), inc=sv)
                kb.emit("scalar", lambda e: e.activation(out=gvar[:], in_=gvar[:], func=AF.Ln,
                                                         bias=eps_s[:, 0:1]), waits=[sv], inc=ss)
                kb.emit("scalar", lambda e: e.activation(out=grs[:], in_=gvar[:], func=AF.Exp, scale=-0.5),
                        inc=ss)
                kb.emit("vector", lambda e, htt=htt: e.tensor_scalar(
                    out=htt, in0=htt, scalar1=grs[:, 0:1], scalar2=None, op0=ALU.mult), waits=[ss], inc=sv)
                kb.emit("vector", lambda e, htt=htt: e.tensor_tensor(out=htt, in0=htt, in1=lng_r[:],
                                                                     op=ALU.mult), inc=sv)
                kb.emit("vector", lambda e, htt=htt: e.tensor_tensor(out=htt, in0=htt, in1=lnb_r[:],
                                                                     op=ALU.add), inc=sv)
                kb.emit("vector", lambda e, htt=htt: e.tensor_scalar(
                    out=htt, in0=htt, scalar1=0.0, scalar2=None, op0=ALU.max), inc=sv)
            def trh2(e):
                for tt in range(4):
                    i = e.transpose(ps_h[:, tt * 128:(tt + 1) * 128], ht[:, tt * 32:(tt + 1) * 32],
                                    identf_s[:])
                return i
            kb.emit("tensor", trh2, waits=[sv, ss], inc=stt)
            kb.emit("vector", lambda e: e.tensor_copy(hfb[:], ps_h[:]), waits=[stt], inc=sv)

            kb.emit("tensor", lambda e: e.matmul(ps_g[0:8, 0:512], gw2T_s[:], hfb[:], start=True, stop=True),
                    waits=[sv, ss], inc=stt)
            kb.emit("scalar", lambda e: e.activation(out=gf[:], in_=ps_g[0:8, 0:512], func=AF.Identity,
                                                     bias=gb2_s[:, 0:1]), waits=[stt], inc=ss)
            for tt in range(4):
                kb.emit("tensor", lambda e, tt=tt: e.transpose(
                    ps_t[:, 0:8], gf[0:8, tt * 128:(tt + 1) * 128], identf_s[0:8, 0:8]),
                    waits=[ss, sv], inc=stt)
                kb.emit("vector", lambda e: e.tensor_copy(gt[:], ps_t[0:128, 0:8])
                        if False else e.tensor_copy(gt[:], ps_t[:, 0:8]), waits=[stt], inc=sv)
                kb.emit("vector", lambda e: e.reduce_max(vmax[:], gt[:], axis=AX), inc=sv)
                kb.emit("vector", lambda e: e.tensor_scalar(
                    out=oh1[:], in0=gt[:], scalar1=vmax[:, 0:1], scalar2=None, op0=ALU.is_ge), inc=sv)
                kb.emit("vector", lambda e: e.tensor_tensor_scan(
                    out=cum[:], data0=ones8_s[:], data1=oh1[:], initial=0.0,
                    op0=ALU.mult, op1=ALU.add), inc=sv)
                kb.emit("vector", lambda e: e.tensor_scalar(
                    out=cum[:], in0=cum[:], scalar1=1.0, scalar2=None, op0=ALU.is_le), inc=sv)
                kb.emit("vector", lambda e: e.tensor_tensor(out=oh1[:], in0=oh1[:], in1=cum[:],
                                                            op=ALU.mult), inc=sv)
                kb.emit("vector", lambda e: e.scalar_tensor_tensor(
                    out=tm8[:], in0=oh1[:], scalar=-1.0e30, in1=gt[:], op0=ALU.mult, op1=ALU.add), inc=sv)
                kb.emit("vector", lambda e: e.reduce_max(v2max[:], tm8[:], axis=AX), inc=sv)
                kb.emit("vector", lambda e: e.tensor_scalar(
                    out=oh2[:], in0=tm8[:], scalar1=v2max[:, 0:1], scalar2=None, op0=ALU.is_ge), inc=sv)
                kb.emit("vector", lambda e: e.tensor_tensor_scan(
                    out=cum[:], data0=ones8_s[:], data1=oh2[:], initial=0.0, op0=ALU.mult, op1=ALU.add),
                    inc=sv)
                kb.emit("vector", lambda e: e.tensor_scalar(
                    out=cum[:], in0=cum[:], scalar1=1.0, scalar2=None, op0=ALU.is_le), inc=sv)
                kb.emit("vector", lambda e: e.tensor_tensor(out=oh2[:], in0=oh2[:], in1=cum[:],
                                                            op=ALU.mult), inc=sv)
                first = (tch == 0 and tt == 0)
                last = (tch == NT // 512 - 1 and tt == 3)

                def mcnt(e, first=first, last=last):
                    e.matmul(ps_cnt[:, 0:1], oh1[:], onescol_s[:], start=first, stop=last,
                             skip_group_check=True)
                    return e.matmul(ps_cnt[:, 1:2], oh2[:], onescol_s[:], start=first, stop=last,
                                    skip_group_check=True)
                kb.emit("tensor", mcnt, waits=[sv], inc=stt)
                if mine:
                    myt = tch * 4 + tt
                    kb.emit("vector", lambda e, myt=myt: e.tensor_copy(
                        oh1my[:, myt * 8:(myt + 1) * 8], oh1[:]), waits=[stt], inc=sv)
                    kb.emit("vector", lambda e, myt=myt: e.tensor_copy(
                        oh2my[:, myt * 8:(myt + 1) * 8], oh2[:]), inc=sv)
                    kb.emit("vector", lambda e: e.tensor_tensor(out=eg[:], in0=v2max[:], in1=vmax[:],
                                                                op=ALU.subtract), inc=sv)
                    kb.emit("scalar", lambda e: e.activation(out=eg[:], in_=eg[:], func=AF.Exp),
                            waits=[sv], inc=ss)
                    kb.emit("vector", lambda e: e.tensor_scalar_add(g2c[:], eg[:], 1.0), waits=[ss], inc=sv)
                    kb.emit("vector", lambda e: e.reciprocal(g1c[:], g2c[:]), inc=sv)
                    kb.emit("vector", lambda e, myt=myt: e.tensor_copy(g1my[:, myt:myt + 1], g1c[:]), inc=sv)
                    kb.emit("vector", lambda e, myt=myt: e.tensor_tensor(
                        out=g2my[:, myt:myt + 1], in0=eg[:], in1=g1c[:], op=ALU.mult), inc=sv)

        kb.emit("vector", lambda e: e.tensor_copy(cnt[:], ps_cnt[:]), waits=[stt], inc=sv)
        kb.emit("vector", lambda e: e.tensor_scalar(
            out=allw[:], in0=cnt[:], scalar1=float(int(3.0 * NT / 8)), scalar2=None, op0=ALU.is_le), inc=sv)
        kb.emit("tensor", lambda e: e.transpose(ps_h[0:2, 0:8], allw[:], identf_s[0:8, 0:8]), waits=[sv], inc=stt)
        kb.emit("vector", lambda e: e.tensor_copy(allwT[0:2, 0:8], ps_h[0:2, 0:8]), waits=[stt], inc=sv)
        kb.dma("sync", allw_d.ap(), allwT[0:2, 0:8], waits=[sv])
        kb.dma("sync", allwT[0:1, 0:16].rearrange("p (a c) -> p a c", a=2), allw_d.ap().unsqueeze(0),
               waits=["dma"])

        kb.emit("tensor", lambda e: e.matmul(ps_a[:, 0:8], ones1_s[:], allwT[0:1, 0:8],
                                             start=True, stop=True), waits=["dma", sv], inc=stt)
        kb.emit("scalar", lambda e: e.copy(out=al0_r[:], in_=ps_a[:, 0:8]), waits=[stt], inc=ss)
        kb.emit("tensor", lambda e: e.matmul(ps_a[:, 0:8], ones1_s[:], allwT[0:1, 8:16],
                                             start=True, stop=True), waits=[ss], inc=stt)
        kb.emit("scalar", lambda e: e.copy(out=al1_r[:], in_=ps_a[:, 0:8]), waits=[stt], inc=ss)

        for myt in range(8):
            kb.emit("vector", lambda e, myt=myt: e.tensor_scalar(
                out=wtk[:], in0=oh1my[:, myt * 8:(myt + 1) * 8], scalar1=g1my[:, myt:myt + 1],
                scalar2=None, op0=ALU.mult), waits=[ss], inc=sv)
            kb.emit("vector", lambda e: e.tensor_tensor(out=wtk[:], in0=wtk[:], in1=al0_r[:], op=ALU.mult),
                    inc=sv)
            kb.emit("vector", lambda e, myt=myt: e.tensor_scalar(
                out=wtk2[:], in0=oh2my[:, myt * 8:(myt + 1) * 8], scalar1=g2my[:, myt:myt + 1],
                scalar2=None, op0=ALU.mult), inc=sv)
            kb.emit("vector", lambda e: e.tensor_tensor(out=wtk2[:], in0=wtk2[:], in1=al1_r[:], op=ALU.mult),
                    inc=sv)
            kb.emit("vector", lambda e: e.tensor_tensor(out=wtk[:], in0=wtk[:], in1=wtk2[:], op=ALU.add),
                    inc=sv)
            kb.emit("tensor", lambda e: e.transpose(ps_g[0:8, 0:128], wtk[:], identf_s[:]),
                    waits=[sv, ss], inc=stt)
            kb.emit("scalar", lambda e, myt=myt: e.copy(out=wT_sb[:, myt * 128:(myt + 1) * 128],
                                                        in_=ps_g[0:8, 0:128]), waits=[stt], inc=ss)

        for tc in range(2):
            kb.emit("tensor", lambda e, tc=tc: e.matmul(ps_t[:], repm_s[:], wT_sb[:, tc * 512:(tc + 1) * 512],
                                                        start=True, stop=True), waits=[ss, sv], inc=stt)
            kb.emit("scalar", lambda e, tc=tc: e.copy(out=wrep[:, tc * 512:(tc + 1) * 512], in_=ps_t[:]),
                    waits=[stt], inc=ss)

        for tc in range(2):
            def mtmp(e, tc=tc):
                for dt_ in range(32):
                    i = e.matmul(ps_a[:], loraA_s[:, dt_ * 128:(dt_ + 1) * 128],
                                 x3b[:, dt_ * TME + tc * 512: dt_ * TME + (tc + 1) * 512],
                                 start=(dt_ == 0), stop=(dt_ == 31))
                return i
            kb.emit("tensor", mtmp, waits=[*n_cast, sv, ss], inc=stt)
            kb.emit("vector", lambda e, tc=tc: e.tensor_tensor(
                out=tmpm[:, tc * 512:(tc + 1) * 512], in0=ps_a[:], in1=wrep[:, tc * 512:(tc + 1) * 512],
                op=ALU.mult), waits=[stt, ss], inc=sv)

        for og in range(8):
            for dt_ in range(32):
                kb.dma("sync", w1T[:, dt_ * 512:(dt_ + 1) * 512],
                       w1bf.ap()[og * 512:(og + 1) * 512, dt_ * 128:(dt_ + 1) * 128],
                       waits=[stt], transpose=True)
            for osub in range(4):
                ot = og * 4 + osub
                for tc in range(2):
                    def mmain(e, osub=osub, tc=tc, ot=ot):
                        for dt_ in range(32):
                            e.matmul(ps_a[:], w1T[:, dt_ * 512 + osub * 128: dt_ * 512 + (osub + 1) * 128],
                                     x3b[:, dt_ * TME + tc * 512: dt_ * TME + (tc + 1) * 512],
                                     start=(dt_ == 0), stop=False)
                        return e.matmul(ps_a[:], loraB_s[:, ot * 128:(ot + 1) * 128],
                                        tmpm[:, tc * 512:(tc + 1) * 512], start=False, stop=True)
                    kb.emit("tensor", mmain, waits=["dma", sv, ss], inc=stt)
                    kb.dma("scalar", x3res[:], x3_d.ap()[ot * 128:(ot + 1) * 128, tc * 512:(tc + 1) * 512],
                           waits=[sv, "dma"])
                    kb.emit("vector", lambda e: e.tensor_tensor(out=zft[:], in0=ps_a[:], in1=x3res[:],
                                                                op=ALU.add), waits=[stt, "dma"], inc=sv)
                    kb.emit("scalar", lambda e: e.square(out=sqs[:], in_=zft[:]), waits=[sv], inc=ss)
                    first = (ot == 0)
                    last = (ot == 31)

                    def mfst(e, first=first, last=last, tc=tc):
                        e.matmul(ps_st[:, tc * 512:(tc + 1) * 512], onescol_s[:], zft[:],
                                 start=first, stop=last, skip_group_check=True)
                        return e.matmul(ps_st2[:, tc * 512:(tc + 1) * 512], onescol_s[:], sqs[:],
                                        start=first, stop=last, skip_group_check=True)
                    kb.emit("tensor", mfst, waits=[ss], inc=stt)

                    def mtr(e):
                        for tt in range(4):
                            i = e.matmul(ps_t[:, tt * 128:(tt + 1) * 128],
                                         zft[:, tt * 128:(tt + 1) * 128], identf_s[:],
                                         start=True, stop=True, is_transpose=True)
                        return i
                    kb.emit("tensor", mtr, waits=[sv], inc=stt)
                    kb.emit("vector", lambda e: e.tensor_copy(sqs[:], ps_t[:]), waits=[stt], inc=sv)
                    kb.dma("sync", zf_d.ap()[tc * 512:(tc + 1) * 512, ot * 128:(ot + 1) * 128]
                           .rearrange("(a p) f -> p a f", p=128), sqs[:].rearrange("p (a f) -> p a f", a=4),
                           waits=[sv])
        n_zf = kb.mark()

        kb.emit("vector", lambda e: e.tensor_scalar_mul(fstat[:], ps_st[:, 0:TME], 1.0 / D),
                waits=[stt], inc=sv)
        kb.emit("vector", lambda e: e.tensor_scalar_mul(fstat2[:], ps_st2[:, 0:TME], 1.0 / D), inc=sv)
        kb.emit("vector", lambda e: e.tensor_tensor(out=sqs[0:1, :], in0=fstat[0:1, 0:512],
                                                    in1=fstat[0:1, 0:512], op=ALU.mult), inc=sv)
        kb.emit("vector", lambda e: e.tensor_tensor(out=fstat2[0:1, 0:512], in0=fstat2[0:1, 0:512],
                                                    in1=sqs[0:1, :], op=ALU.subtract), inc=sv)
        kb.emit("vector", lambda e: e.tensor_tensor(out=sqs[0:1, :], in0=fstat[0:1, 512:1024],
                                                    in1=fstat[0:1, 512:1024], op=ALU.mult), inc=sv)
        kb.emit("vector", lambda e: e.tensor_tensor(out=fstat2[0:1, 512:1024], in0=fstat2[0:1, 512:1024],
                                                    in1=sqs[0:1, :], op=ALU.subtract), inc=sv)
        kb.emit("scalar", lambda e: e.activation(out=fstat2[:], in_=fstat2[:], func=AF.Ln,
                                                 bias=eps_s[0:1, 0:1]), waits=[sv], inc=ss)
        kb.emit("scalar", lambda e: e.activation(out=fstat2[:], in_=fstat2[:], func=AF.Exp, scale=-0.5),
                inc=ss)
        kb.emit("vector", lambda e: e.tensor_scalar_mul(fstat[:], fstat[:], -1.0), waits=[ss], inc=sv)
        for tt in range(8):
            def mtrs(e, tt=tt):
                e.matmul(ps_t[:, 0:1], fstat[0:1, tt * 128:(tt + 1) * 128], identf_s[0:1, 0:1],
                         start=True, stop=True, is_transpose=True)
                return e.matmul(ps_t[:, 1:2], fstat2[0:1, tt * 128:(tt + 1) * 128], identf_s[0:1, 0:1],
                                start=True, stop=True, is_transpose=True)
            kb.emit("tensor", mtrs, waits=[sv, ss], inc=stt)
            kb.emit("scalar", lambda e, tt=tt: e.copy(out=fmuT[:, tt:tt + 1], in_=ps_t[:, 0:1]),
                    waits=[stt], inc=ss)
            kb.emit("scalar", lambda e, tt=tt: e.copy(out=frsT[:, tt:tt + 1], in_=ps_t[:, 1:2]), inc=ss)

        for tt in range(8):
            kb.dma("sync", outt[:], zf_d.ap()[tt * 128:(tt + 1) * 128, :], waits=[*n_zf, ss, sv, "dma"])
            kb.emit("vector", lambda e, tt=tt: e.tensor_scalar(
                out=outt[:], in0=outt[:], scalar1=fmuT[:, tt:tt + 1], scalar2=None, op0=ALU.add),
                waits=["dma", ss], inc=sv)
            kb.emit("vector", lambda e, tt=tt: e.tensor_scalar(
                out=outt[:], in0=outt[:], scalar1=frsT[:, tt:tt + 1], scalar2=None, op0=ALU.mult), inc=sv)
            kb.dma("sync", out_o[tt * 128:(tt + 1) * 128, :], outt[:], waits=[sv])

        kb.emit("sync", lambda e: e.nop(), waits=["dma"])
        kb.build(es)
    return nc


class Ar:

    def __init__(self, nc, es, name, cols_f32):
        self.t = es.enter_context(nc.sbuf_tensor(name, [128, cols_f32], F32))
        self.cols = cols_f32
        self.off = 0

    def reset(self):
        self.off = 0

    def at(self, off, cols, dt=F32):
        f32c = cols if dt == F32 or dt == I32 else (cols + 1) // 2
        ap = self.t.ap()[:, off:off + f32c]
        if dt == BF16:
            ap = ap.bitcast(BF16)
        elif dt == I32:
            ap = ap.bitcast(I32)
        return ap

    def take(self, cols, dt=F32):
        f32c = cols if dt == F32 or dt == I32 else (cols + 1) // 2
        assert self.off + f32c <= self.cols, f"arena overflow {self.off}+{f32c}>{self.cols}"
        ap = self.t.ap()[:, self.off:self.off + f32c]
        self.off += f32c
        if dt == BF16:
            ap = ap.bitcast(BF16)
        elif dt == I32:
            ap = ap.bitcast(I32)
        return ap


def _barrier(kb, sems):
    waits = ["dma"] + [s for s in sems]
    for eng in ("sync", "vector", "scalar", "tensor", "gpsimd"):
        kb.emit(eng, lambda e: None, waits=waits, dma_op=True)


def _mega_dequant(kb, nc, AR, io):
    qw, qz, sc, gidx = io["qw"], io["qz"], io["sc"], io["gidx"]
    wsl_in = io["wsl_in"]
    PS2a, PS2b = io["PS2a"], io["PS2b"]
    identf_s, iota_s, shf_s = io["identf_s"], io["identf_s"], io["shf_s"]
    sv, ss, stt, sg = io["sv"], io["ss"], io["stt"], io["sg"]
    iota_s = io["iota_s"]
    AR.reset()
    qz_s = AR.take(64, I32)[0:32, :]
    zi = AR.take(512, I32)[0:32, :]
    zf = AR.take(512)[0:32, :]
    zs = AR.take(512)[0:32, :]
    sc_s = AR.take(512)[0:32, :]
    gix = AR.take(1, I32)
    gixf = AR.take(1)
    oh = AR.take(32)
    ohT = AR.take(128)[0:32, :]
    qrep = AR.take(512, I32)
    qi = AR.take(512, I32)
    qf = AR.take(512)
    wtmp = AR.take(512)
    wout = AR.take(512, BF16)
    ps_oh = PS2b[0:32, 0:128]
    ps_s1 = PS2a[:, 0:512]
    ps_s2 = PS2a[:, 512:1024]

    kb.dma("sync", qz_s, qz)
    kb.dma("sync", sc_s, sc)
    n0 = kb.mark()
    for s_ in range(8):
        kb.emit("vector", lambda e, s_=s_: e.tensor_scalar(
            out=zi.rearrange("p (c e) -> p c e", e=8)[:, :, s_], in0=qz_s,
            scalar1=4 * s_, scalar2=15, op0=ALU.logical_shift_right, op1=ALU.bitwise_and),
            waits=n0)
    kb.emit("vector", lambda e: e.tensor_scalar_add(zi, zi, 1))
    kb.emit("vector", lambda e: e.tensor_copy(zf, zi))
    kb.emit("vector", lambda e: e.tensor_tensor(out=zs, in0=zf, in1=sc_s, op=ALU.mult))

    for ki in range(32):
        kb.dma("sync", gix, gidx[ki * 128:(ki + 1) * 128, :], waits=[sv])
        kb.dma("gpsimd", qrep, qw[ki * 16:(ki + 1) * 16, :].unsqueeze(1).broadcast_to([16, 8, 512]),
               waits=[sv])
        kb.emit("vector", lambda e: e.tensor_copy(gixf, gix), waits=["dma", stt])
        kb.emit("vector", lambda e: e.tensor_scalar(
            out=oh, in0=iota_s[:], scalar1=gixf[:, 0:1], scalar2=None, op0=ALU.is_equal), flush=True)
        kb.emit("tensor", lambda e: e.transpose(ps_oh, oh, identf_s[:]), waits=[sv, ss], inc=stt)
        kb.emit("scalar", lambda e: e.copy(out=ohT, in_=ps_oh), waits=[stt])

        def gsc(e):
            e.matmul(ps_s1, ohT, sc_s, start=True, stop=True)
            return e.matmul(ps_s2, ohT, zs, start=True, stop=True)
        kb.emit("tensor", gsc, waits=[ss, sv], inc=stt)
        kb.emit("vector", lambda e: e.tensor_scalar(
            out=qi, in0=qrep, scalar1=shf_s[:, 0:1], scalar2=15,
            op0=ALU.logical_shift_right, op1=ALU.bitwise_and), waits=["dma", stt])
        kb.emit("vector", lambda e: e.tensor_copy(qf, qi))
        kb.emit("vector", lambda e: e.tensor_tensor(out=wtmp, in0=qf, in1=ps_s1, op=ALU.mult))
        kb.emit("vector", lambda e: e.tensor_tensor(out=wout, in0=wtmp, in1=ps_s2, op=ALU.subtract),
                waits=["dma"])
        kb.dma("sync", wsl_in.ap()[ki * 128:(ki + 1) * 128, :], wout, waits=[sv])

    kb.emit("gpsimd", lambda e: e.collective_compute(
        "AllGather", ALU.bypass, replica_groups=[[0, 1, 2, 3, 4, 5, 6, 7]],
        ins=[wsl_in.ap().opt()], outs=[io["wsl_out"].ap().opt()]),
        waits=["dma", sv, ss, stt])
    io["n_wcol"] = io["sg"].n


def _mega_attn(kb, nc, AR, io):
    xbf, wqbf, wkbf, wvbf = io["xbf"], io["wqbf"], io["wkbf"], io["wvbf"]
    wqb, wkb, wvb = io["wqb"], io["wkb"], io["wvb"]
    x1c_in = io["x1c_in"]
    PS4, PS2a, PS2b = io["PS4"], io["PS2a"], io["PS2b"]
    identb_s, mask_s = io["identb_s"], io["mask_s"]
    sv, ss, stt, sg = io["sv"], io["ss"], io["stt"], io["sg"]
    SEMS = (sv, ss, stt, sg)
    n_cast0 = io["n_cast0"]
    _barrier(kb, SEMS)
    AR.reset()
    xT = AR.take(32 * 512, BF16)
    wqT = AR.take(32 * 512, BF16)
    wkT = AR.take(32 * 128, BF16)
    wvT = AR.take(32 * 128, BF16)
    q_sb = AR.take(16 * 2048, BF16)
    k_sb = AR.take(2048, BF16)
    v_sb = AR.take(2048, BF16)
    vkd = AR.take(2048, BF16)
    p_sb = AR.take(2048, BF16)
    pt8 = AR.take(1024, BF16)
    att_sb = AR.take(128, BF16)
    nmax = AR.take(1)
    lrow = AR.take(1)
    rl = AR.take(1)
    qb_s = AR.take(16)
    kb_s = AR.take(1)
    vb_s = AR.take(1)
    ps_s = PS4
    ps_pj = PS2a[:, 0:512]
    ps_pj2 = PS2a[:, 512:1024]
    ps_pt = PS2b[:, 0:512].bitcast(BF16)
    ps_att = PS2b[:, 512:640]

    kb.dma("sync", qb_s, wqb)
    kb.dma("sync", kb_s, wkb)
    kb.dma("sync", vb_s, wvb)
    for dt_ in range(32):
        kb.dma("sync", wkT[:, dt_ * 128:(dt_ + 1) * 128], wkbf.ap()[:, dt_ * 128:(dt_ + 1) * 128],
               waits=n_cast0, transpose=True)
        kb.dma("sync", wvT[:, dt_ * 128:(dt_ + 1) * 128], wvbf.ap()[:, dt_ * 128:(dt_ + 1) * 128],
               waits=n_cast0, transpose=True)
    kb.emit("vector", lambda e: e.tensor_scalar_mul(kb_s, kb_s, RS2), waits=["dma"])

    for tch in range(4):
        t0 = tch * 512
        for dt_ in range(32):
            kb.dma("sync", xT[:, dt_ * 512:(dt_ + 1) * 512],
                   xbf.ap()[t0:t0 + 512, dt_ * 128:(dt_ + 1) * 128],
                   waits=[*n_cast0, stt], transpose=True)

        def mmk(e):
            for dt_ in range(32):
                i = e.matmul(ps_pj, wkT[:, dt_ * 128:(dt_ + 1) * 128],
                             xT[:, dt_ * 512:(dt_ + 1) * 512], start=(dt_ == 0), stop=(dt_ == 31))
            return i
        kb.emit("tensor", mmk, waits=["dma", sv, ss], inc=stt)
        kb.emit("scalar", lambda e, t0=t0: e.activation(
            out=k_sb[:, t0:t0 + 512], in_=ps_pj, func=AF.Identity, bias=kb_s[:, 0:1], scale=RS2),
            waits=[stt])

        def mmv(e):
            for dt_ in range(32):
                i = e.matmul(ps_pj2, wvT[:, dt_ * 128:(dt_ + 1) * 128],
                             xT[:, dt_ * 512:(dt_ + 1) * 512], start=(dt_ == 0), stop=(dt_ == 31))
            return i
        kb.emit("tensor", mmv, waits=["dma"], inc=stt)
        kb.emit("scalar", lambda e, t0=t0: e.activation(
            out=v_sb[:, t0:t0 + 512], in_=ps_pj2, func=AF.Identity, bias=vb_s[:, 0:1]),
            waits=[stt])

        for og in range(4):
            for dt_ in range(32):
                kb.dma("sync", wqT[:, dt_ * 512:(dt_ + 1) * 512],
                       wqbf.ap()[og * 512:(og + 1) * 512, dt_ * 128:(dt_ + 1) * 128],
                       waits=[*n_cast0, stt], transpose=True)
            for osub in range(4):
                ot = og * 4 + osub
                ps = ps_pj if (osub % 2 == 0) else ps_pj2

                def mmq(e, osub=osub, ps=ps):
                    for dt_ in range(32):
                        i = e.matmul(ps, wqT[:, dt_ * 512 + osub * 128: dt_ * 512 + (osub + 1) * 128],
                                     xT[:, dt_ * 512:(dt_ + 1) * 512], start=(dt_ == 0), stop=(dt_ == 31))
                    return i
                kb.emit("tensor", mmq, waits=["dma", ss], inc=stt)
                kb.emit("scalar", lambda e, ot=ot, t0=t0, ps=ps: e.activation(
                    out=q_sb[:, ot * 2048 + t0: ot * 2048 + t0 + 512], in_=ps,
                    func=AF.Identity, bias=qb_s[:, ot:ot + 1]), waits=[stt])

    for rnd in range(2):
        def tr8(e, rnd=rnd):
            for j in range(8):
                i = e.transpose(ps_pt[:, j * 128:(j + 1) * 128],
                                v_sb[:, (rnd * 8 + j) * 128:(rnd * 8 + j + 1) * 128], identb_s[:])
            return i
        kb.emit("tensor", tr8, waits=[ss, sv], inc=stt)
        kb.emit("vector", lambda e, rnd=rnd: e.tensor_copy(vkd[:, rnd * 1024:(rnd + 1) * 1024], ps_pt),
                waits=[stt])

    n_exp_prev = ss.n
    n_trp_prev = 0
    n_store_prev = 0
    dq_sync = kb.dsem["sync"]
    for qt in range(16):
        kmax = (qt + 1) * 128
        nch = (kmax + 511) // 512
        sect = qt // 8
        for h_ in range(16):
            def mms(e, qt=qt, h_=h_, kmax=kmax, nch=nch):
                for ch in range(nch):
                    c0 = ch * 512
                    c1 = min(kmax, c0 + 512)
                    i = e.matmul(ps_s[:, c0:c1],
                                 q_sb[:, h_ * 2048 + qt * 128: h_ * 2048 + (qt + 1) * 128],
                                 k_sb[:, c0:c1], start=True, stop=True)
                return i
            kb.emit("tensor", mms, waits=[(ss, n_exp_prev)], inc=stt)
            n_mms = stt.n
            kb.emit("vector", lambda e, qt=qt, kmax=kmax: e.tensor_tensor(
                out=ps_s[:, qt * 128:kmax], in0=ps_s[:, qt * 128:kmax], in1=mask_s[:], op=ALU.add),
                waits=[(stt, n_mms)])
            kb.emit("vector", lambda e, kmax=kmax: e.reduce_max(
                nmax, ps_s[:, 0:kmax], axis=AX, negate=True))
            n_max = sv.n
            kb.emit("scalar", lambda e, kmax=kmax: e.activation(
                out=p_sb[:, 0:kmax], in_=ps_s[:, 0:kmax], func=AF.Exp,
                bias=nmax[:, 0:1], accum_out=lrow), waits=[(sv, n_max), (stt, n_trp_prev)])
            n_exp_prev = ss.n
            kb.emit("vector", lambda e: e.reciprocal(rl, lrow), waits=[(ss, n_exp_prev)])
            kb.emit("vector", lambda e, kmax=kmax: e.tensor_scalar(
                out=p_sb[:, 0:kmax], in0=p_sb[:, 0:kmax], scalar1=rl[:, 0:1], scalar2=None,
                op0=ALU.mult), flush=True)
            n_norm = sv.n
            n_av_last = stt.n
            for g0 in range(0, qt + 1, 8):
                g1 = min(qt + 1, g0 + 8)

                def trp(e, g0=g0, g1=g1):
                    for j in range(g0, g1):
                        i = e.transpose(ps_pt[:, (j - g0) * 128:(j - g0 + 1) * 128],
                                        p_sb[:, j * 128:(j + 1) * 128], identb_s[:])
                    return i
                kb.emit("tensor", trp, waits=[(sv, n_norm), (ss, ss.n)], inc=stt)
                n_trp_prev = stt.n
                kb.emit("scalar", lambda e, g0=g0, g1=g1: e.copy(
                    out=pt8[:, 0:(g1 - g0) * 128], in_=ps_pt[:, 0:(g1 - g0) * 128]),
                    waits=[(stt, n_trp_prev)])
                n_ptev = ss.n

                def mav(e, g0=g0, g1=g1, qt=qt):
                    for j in range(g0, g1):
                        i = e.matmul(ps_att, vkd[:, j * 128:(j + 1) * 128],
                                     pt8[:, (j - g0) * 128:(j - g0 + 1) * 128],
                                     start=(j == 0), stop=(j == qt))
                    return i
                kb.emit("tensor", mav, waits=[(ss, n_ptev)], inc=stt)
                n_av_last = stt.n
            kb.emit("scalar", lambda e: e.copy(out=att_sb, in_=ps_att),
                    waits=[(stt, n_av_last), (dq_sync, n_store_prev)])
            n_attev = ss.n
            kb.dma("sync", x1c_in.ap()[sect * 2048 + h_ * 128: sect * 2048 + (h_ + 1) * 128,
                                       (qt % 8) * 128:(qt % 8 + 1) * 128], att_sb,
                   waits=[(ss, n_attev)])
            n_store_prev = dq_sync.n

    kb.emit("gpsimd", lambda e: e.collective_compute(
        "AllGather", ALU.bypass, replica_groups=[[0, 1], [2, 3], [4, 5], [6, 7]],
        ins=[x1c_in.ap().opt()], outs=[io["x1c_out"].ap().opt()]),
        waits=["dma", sv, ss, stt])
    io["n_x1col"] = io["sg"].n


def _mega_wo(kb, nc, AR, io):
    wobf, wob, xmy = io["wobf"], io["wob"], io["xmy"]
    x1c_out, z_d, x1_d = io["x1c_out"], io["z_d"], io["x1_d"]
    PS4, PS2a, PS2b = io["PS4"], io["PS2a"], io["PS2b"]
    identf_s, ones1_s, onescol_s, eps_s = io["identf_s"], io["ones1_s"], io["onescol_s"], io["eps_s"]
    xidx_s = io["xidx_s"]
    sv, ss, stt, sg = io["sv"], io["ss"], io["stt"], io["sg"]
    SEMS = (sv, ss, stt, sg)
    n_cast0 = io["n_cast0"]
    n_x1col = io["n_x1col"]
    _barrier(kb, SEMS)
    AR.reset()
    a_sb = AR.take(32 * TME, BF16)
    woT = [AR.take(32 * 512, BF16), AR.take(32 * 512, BF16)]
    xr_sb = [AR.take(8 * 128), AR.take(8 * 128)]
    xres = [AR.take(TME), AR.take(TME)]
    ztl = [AR.take(TME), AR.take(TME)]
    sq = [AR.take(TME), AR.take(TME)]
    bias_s = AR.take(32)
    mu = AR.take(TME)[0:1, :]
    var = AR.take(TME)[0:1, :]
    rstd = AR.take(TME)[0:1, :]
    murep = AR.take(TME)
    rsrep = AR.take(TME)
    ps_o = [PS2a, PS2b]
    ps_rep = PS2b
    ps_st = PS4[0:1, 0:1024]
    ps_st2 = PS4[0:1, 1024:2048]

    kb.dma("sync", bias_s, wob)
    for dt_ in range(32):
        kb.idma(a_sb[:, dt_ * TME:(dt_ + 1) * TME], x1c_out.ap(), xidx_s[:, dt_:dt_ + 1],
                waits=[(sg, n_x1col)])
    n_asb = kb.mark()
    dq_sync = kb.dsem["sync"]
    n_wload = [0, 0]
    n_evac = [0, 0]
    n_mmo = [0, 0]
    n_mmw = [0, 0]
    n_sq_p = [0, 0]
    st_z = 0
    for dt_ in range(32):
        kb.dma("sync", woT[0][:, dt_ * 512:(dt_ + 1) * 512],
               wobf.ap()[0:512, dt_ * 128:(dt_ + 1) * 128], waits=n_cast0, transpose=True)
    n_wload[0] = dq_sync.n
    for og in range(8):
        wp = og % 2
        if og + 1 < 8:
            for dt_ in range(32):
                kb.dma("sync", woT[1 - wp][:, dt_ * 512:(dt_ + 1) * 512],
                       wobf.ap()[(og + 1) * 512:(og + 2) * 512, dt_ * 128:(dt_ + 1) * 128],
                       waits=[*n_cast0, (stt, n_mmw[1 - wp])], transpose=True)
            n_wload[1 - wp] = dq_sync.n
        for osub in range(4):
            ot = og * 4 + osub
            pp = ot % 2
            psA = ps_o[pp]
            for tt in range(8):
                kb.dma("scalar", xr_sb[pp][:, tt * 128:(tt + 1) * 128],
                       xmy[tt * 128:(tt + 1) * 128, ot * 128:(ot + 1) * 128],
                       waits=[(stt, n_mmo[pp])])
            n_xr = kb.mark()

            def trx(e, pp=pp):
                for tt in range(8):
                    i = e.matmul(psA[:, tt * 128:(tt + 1) * 128] if False else
                                 ps_o[pp][:, tt * 128:(tt + 1) * 128],
                                 xr_sb[pp][:, tt * 128:(tt + 1) * 128],
                                 identf_s[:], start=True, stop=True, is_transpose=True)
                return i
            kb.emit("tensor", trx, waits=[*n_xr, (sv, n_evac[pp]), *n_asb], inc=stt)
            n_trx = stt.n
            kb.emit("vector", lambda e, pp=pp: e.tensor_copy(xres[pp], ps_o[pp][:, 0:1024]),
                    waits=[(stt, n_trx)])
            n_xres = sv.n

            def mmo(e, osub=osub, pp=pp, wp=wp):
                for tc in range(2):
                    for dt_ in range(32):
                        i = e.matmul(ps_o[pp][:, tc * 512:(tc + 1) * 512],
                                     woT[wp][:, dt_ * 512 + osub * 128: dt_ * 512 + (osub + 1) * 128],
                                     a_sb[:, dt_ * TME + tc * 512: dt_ * TME + (tc + 1) * 512],
                                     start=(dt_ == 0), stop=(dt_ == 31))
                return i
            kb.emit("tensor", mmo, waits=[(dq_sync, n_wload[wp]), (sv, n_xres)], inc=stt)
            n_mmo[pp] = stt.n
            n_mmw[wp] = stt.n
            kb.emit("vector", lambda e, ot=ot, pp=pp: e.scalar_tensor_tensor(
                out=ztl[pp][:, 0:512], in0=ps_o[pp][:, 0:512], scalar=bias_s[:, ot:ot + 1],
                in1=xres[pp][:, 0:512], op0=ALU.add, op1=ALU.add),
                waits=[(stt, n_mmo[pp]), (dq_sync, st_z), (ss, n_sq_p[pp])])
            kb.emit("vector", lambda e, ot=ot, pp=pp: e.scalar_tensor_tensor(
                out=ztl[pp][:, 512:1024], in0=ps_o[pp][:, 512:1024], scalar=bias_s[:, ot:ot + 1],
                in1=xres[pp][:, 512:1024], op0=ALU.add, op1=ALU.add))
            n_z = sv.n
            kb.emit("scalar", lambda e, pp=pp: e.square(out=sq[pp], in_=ztl[pp]), waits=[(sv, n_z)])
            n_sq = ss.n
            n_sq_p[pp] = n_sq

            def mst(e, ot=ot, pp=pp):
                for tc in range(2):
                    e.matmul(ps_st[:, tc * 512:(tc + 1) * 512], onescol_s[:],
                             ztl[pp][:, tc * 512:(tc + 1) * 512], start=(ot == 0), stop=(ot == 31),
                             skip_group_check=True)
                    i = e.matmul(ps_st2[:, tc * 512:(tc + 1) * 512], onescol_s[:],
                                 sq[pp][:, tc * 512:(tc + 1) * 512], start=(ot == 0), stop=(ot == 31),
                                 skip_group_check=True)
                return i
            kb.emit("tensor", mst, waits=[(ss, n_sq)], inc=stt)
            n_mst = stt.n
            n_evac[pp] = n_z
            kb.dma("sync", z_d.ap()[ot * 128:(ot + 1) * 128, :], ztl[pp], waits=[(ss, n_sq)])
            st_z = dq_sync.n

    _mega_ln_stream(kb, dict(sq=sq[0], mu=mu, var=var, rstd=rstd, murep=murep, rsrep=rsrep,
                             ones1_s=ones1_s, ps_st=ps_st, ps_st2=ps_st2, ps_rep=ps_rep,
                             ztl=ztl[0], eps_s=eps_s), SEMS, x1_d.ap(), z_d)


def _mega_ln_stream(kb, bufs, sems, out_dram, z_d, per_tile=None):
    sv, ss, stt, sg = sems
    sq, mu, var, rstd = bufs["sq"], bufs["mu"], bufs["var"], bufs["rstd"]
    murep, rsrep = bufs["murep"], bufs["rsrep"]
    ones1_s = bufs["ones1_s"]
    ps_st, ps_st2, ps_rep = bufs["ps_st"], bufs["ps_st2"], bufs["ps_rep"]
    ztl = bufs["ztl"]
    eps_s = bufs["eps_s"]
    tme = TME
    nq = tme // 512
    kb.emit("vector", lambda e: e.engine_nop(), waits=["dma"])
    kb.emit("vector", lambda e: e.tensor_scalar_mul(mu, ps_st, 1.0 / D), waits=[stt])
    kb.emit("vector", lambda e: e.tensor_scalar_mul(var, ps_st2, 1.0 / D))
    kb.emit("vector", lambda e: e.tensor_tensor(out=sq[0:1, 0:tme], in0=mu, in1=mu, op=ALU.mult))
    kb.emit("vector", lambda e: e.tensor_tensor(out=var, in0=var, in1=sq[0:1, 0:tme], op=ALU.subtract))
    kb.emit("scalar", lambda e: e.activation(out=var, in_=var, func=AF.Ln, bias=eps_s[0:1, 0:1]),
            waits=[sv])
    kb.emit("scalar", lambda e: e.activation(out=rstd, in_=var, func=AF.Exp, scale=-0.5))
    kb.emit("vector", lambda e: e.tensor_scalar_mul(mu, mu, -1.0), waits=[ss])

    def mrep1(e):
        for tc in range(nq):
            i = e.matmul(ps_rep[:, tc * 512:(tc + 1) * 512], ones1_s[:], mu[0:1, tc * 512:(tc + 1) * 512],
                         start=True, stop=True)
        return i
    kb.emit("tensor", mrep1, waits=[sv, ss], inc=stt)
    kb.emit("scalar", lambda e: e.copy(out=murep, in_=ps_rep[:, 0:tme]), waits=[stt])

    def mrep2(e):
        for tc in range(nq):
            i = e.matmul(ps_rep[:, tc * 512:(tc + 1) * 512], ones1_s[:], rstd[0:1, tc * 512:(tc + 1) * 512],
                         start=True, stop=True)
        return i
    kb.emit("tensor", mrep2, waits=[ss], inc=stt)
    kb.emit("scalar", lambda e: e.copy(out=rsrep, in_=ps_rep[:, 0:tme]), waits=[stt])

    for dt_ in range(32):
        kb.dma("sync", ztl, z_d.ap()[dt_ * 128:(dt_ + 1) * 128, :], waits=[sv, ss, "dma"])
        kb.emit("vector", lambda e: e.tensor_tensor(out=ztl, in0=ztl, in1=murep, op=ALU.add),
                waits=["dma", ss])
        kb.emit("vector", lambda e: e.tensor_tensor(out=ztl, in0=ztl, in1=rsrep, op=ALU.mult))
        if per_tile is not None:
            per_tile(dt_)
        if out_dram is not None:
            kb.dma("sync", out_dram[dt_ * 128:(dt_ + 1) * 128, :], ztl, waits=[sv])


def _mega_ffn(kb, nc, AR, io):
    x1_d, wsl_out, fbias = io["x1_d"], io["wsl_out"], io["fbias"]
    z_d, x2c_in, xpjbf = io["z_d"], io["x2c_in"], io["xpjbf"]
    PS4, PS2a, PS2b = io["PS4"], io["PS2a"], io["PS2b"]
    ones1_s, onescol_s, eps_s = io["ones1_s"], io["onescol_s"], io["eps_s"]
    sv, ss, stt, sg = io["sv"], io["ss"], io["stt"], io["sg"]
    SEMS = (sv, ss, stt, sg)
    n_cast0 = io["n_cast0"]
    n_wcol = io["n_wcol"]
    _barrier(kb, SEMS)
    AR.reset()
    x1b = AR.take(32 * TME, BF16)
    wblk = [AR.take(32 * 512, BF16), AR.take(32 * 512, BF16)]
    x1t = [AR.take(TME), AR.take(TME)]
    ztl = [AR.take(TME), AR.take(TME)]
    ztlb = AR.take(TME, BF16)
    sq = [AR.take(TME), AR.take(TME)]
    bias_s = AR.take(32)
    mu = AR.take(TME)[0:1, :]
    var = AR.take(TME)[0:1, :]
    rstd = AR.take(TME)[0:1, :]
    murep = AR.take(TME)
    rsrep = AR.take(TME)
    xpj1 = AR.take(32 * 128, BF16)
    xpj2 = AR.take(32 * 32, BF16)
    xpev = AR.take(TME)
    ps_o = [PS2a, PS2b]
    ps_st = PS4[0:1, 0:1024]
    ps_st2 = PS4[0:1, 1024:2048]
    ps_rep = PS2b
    ps_xp = PS4[:, 0:1024]
    ps_xp2 = PS2b[0:32, 0:1024]

    kb.dma("sync", bias_s, fbias)
    for dt_ in range(32):
        kb.dma("sync", xpj1[:, dt_ * 128:(dt_ + 1) * 128], xpjbf.ap()[0:128, dt_ * 128:(dt_ + 1) * 128],
               waits=n_cast0, transpose=True)
        kb.dma("sync", xpj2[:, dt_ * 32:(dt_ + 1) * 32], xpjbf.ap()[128:160, dt_ * 128:(dt_ + 1) * 128],
               waits=n_cast0, transpose=True)
    for dt_ in range(32):
        kb.dma("sync", x1t[0], x1_d.ap()[dt_ * 128:(dt_ + 1) * 128, :], waits=[sv])
        kb.emit("vector", lambda e, dt_=dt_: e.tensor_copy(x1b[:, dt_ * TME:(dt_ + 1) * TME], x1t[0]),
                waits=["dma"])
    n_x1b = sv.n

    dq_sync = kb.dsem["sync"]
    dq_sc = kb.dsem["scalar"]
    n_wload = [0, 0]
    n_mmf = [0, 0]
    n_mmw = [0, 0]
    n_evac = [0, 0]
    n_x1t = [0, 0]
    n_sq_p = [0, 0]
    st_z = 0
    for dt_ in range(32):
        kb.dma("sync", wblk[0][:, dt_ * 512:(dt_ + 1) * 512],
               wsl_out.ap()[0 * D + dt_ * 128: 0 * D + (dt_ + 1) * 128, :],
               waits=[(sg, n_wcol)])
    n_wload[0] = dq_sync.n
    for og in range(8):
        wp = og % 2
        if og + 1 < 8:
            for dt_ in range(32):
                kb.dma("sync", wblk[1 - wp][:, dt_ * 512:(dt_ + 1) * 512],
                       wsl_out.ap()[(og + 1) * D + dt_ * 128: (og + 1) * D + (dt_ + 1) * 128, :],
                       waits=[(sg, n_wcol), (stt, n_mmw[1 - wp])])
            n_wload[1 - wp] = dq_sync.n
        for osub in range(4):
            ot = og * 4 + osub
            pp = ot % 2
            kb.dma("scalar", x1t[pp], x1_d.ap()[ot * 128:(ot + 1) * 128, :],
                   waits=[(sv, max(n_evac[pp], n_x1b))])
            n_x1t[pp] = dq_sc.n

            def mmf(e, osub=osub, pp=pp, wp=wp):
                for tc in range(2):
                    for dt_ in range(32):
                        i = e.matmul(ps_o[pp][:, tc * 512:(tc + 1) * 512],
                                     wblk[wp][:, dt_ * 512 + osub * 128: dt_ * 512 + (osub + 1) * 128],
                                     x1b[:, dt_ * TME + tc * 512: dt_ * TME + (tc + 1) * 512],
                                     start=(dt_ == 0), stop=(dt_ == 31))
                return i
            kb.emit("tensor", mmf, waits=[(dq_sync, n_wload[wp]), (sv, n_x1b), (sv, n_evac[pp])],
                    inc=stt)
            n_mmf[pp] = stt.n
            n_mmw[wp] = stt.n
            kb.emit("vector", lambda e, ot=ot, pp=pp: e.scalar_tensor_tensor(
                out=ztl[pp][:, 0:512], in0=ps_o[pp][:, 0:512], scalar=bias_s[:, ot:ot + 1],
                in1=x1t[pp][:, 0:512], op0=ALU.add, op1=ALU.add),
                waits=[(stt, n_mmf[pp]), (dq_sc, n_x1t[pp]), (dq_sync, st_z), (ss, n_sq_p[pp])])
            kb.emit("vector", lambda e, ot=ot, pp=pp: e.scalar_tensor_tensor(
                out=ztl[pp][:, 512:1024], in0=ps_o[pp][:, 512:1024], scalar=bias_s[:, ot:ot + 1],
                in1=x1t[pp][:, 512:1024], op0=ALU.add, op1=ALU.add))
            n_z = sv.n
            n_evac[pp] = n_z
            kb.emit("scalar", lambda e, pp=pp: e.square(out=sq[pp], in_=ztl[pp]), waits=[(sv, n_z)])
            n_sq = ss.n
            n_sq_p[pp] = n_sq

            def mst(e, ot=ot, pp=pp):
                for tc in range(2):
                    e.matmul(ps_st[:, tc * 512:(tc + 1) * 512], onescol_s[:],
                             ztl[pp][:, tc * 512:(tc + 1) * 512], start=(ot == 0), stop=(ot == 31),
                             skip_group_check=True)
                    i = e.matmul(ps_st2[:, tc * 512:(tc + 1) * 512], onescol_s[:],
                                 sq[pp][:, tc * 512:(tc + 1) * 512], start=(ot == 0), stop=(ot == 31),
                                 skip_group_check=True)
                return i
            kb.emit("tensor", mst, waits=[(ss, n_sq)], inc=stt)
            kb.dma("sync", z_d.ap()[ot * 128:(ot + 1) * 128, :], ztl[pp], waits=[(ss, n_sq)])
            st_z = dq_sync.n

    def per_tile(dt_):
        kb.dma("sync", x2c_in.ap()[dt_ * 128:(dt_ + 1) * 128, :], ztl[0], waits=[sv])
        kb.emit("vector", lambda e: e.tensor_copy(ztlb, ztl[0]))

        def mxp(e, dt_=dt_):
            for tc in range(2):
                e.matmul(ps_xp[:, tc * 512:(tc + 1) * 512], xpj1[:, dt_ * 128:(dt_ + 1) * 128],
                         ztlb[:, tc * 512:(tc + 1) * 512], start=(dt_ == 0), stop=(dt_ == 31),
                         skip_group_check=True)
                i = e.matmul(ps_xp2[:, tc * 512:(tc + 1) * 512], xpj2[:, dt_ * 32:(dt_ + 1) * 32],
                             ztlb[:, tc * 512:(tc + 1) * 512], start=(dt_ == 0), stop=(dt_ == 31),
                             skip_group_check=True)
            return i
        kb.emit("tensor", mxp, waits=["dma", sv, ss], inc=stt)

    _mega_ln_stream(kb, dict(sq=sq[0], mu=mu, var=var, rstd=rstd, murep=murep, rsrep=rsrep,
                             ones1_s=ones1_s, ps_st=ps_st, ps_st2=ps_st2, ps_rep=ps_rep,
                             ztl=ztl[0], eps_s=eps_s), SEMS, None, z_d, per_tile=per_tile)
    kb.emit("scalar", lambda e: e.copy(out=xpev, in_=ps_xp), waits=[stt])
    kb.dma("sync", x2c_in.ap()[D:D + 128, :], xpev, waits=[ss])
    kb.emit("scalar", lambda e: e.copy(out=xpev[0:32, :], in_=ps_xp2), waits=["dma"])
    kb.dma("sync", x2c_in.ap()[D + 128:D + 160, :], xpev[0:32, :], waits=[ss])

    kb.emit("gpsimd", lambda e: e.collective_compute(
        "AllGather", ALU.bypass, replica_groups=[[0, 1], [2, 3], [4, 5], [6, 7]],
        ins=[x2c_in.ap().opt()], outs=[io["x2c_out"].ap().opt()]),
        waits=["dma", sv, ss, stt])
    io["n_x2col"] = io["sg"].n


def _mega_ssm(kb, nc, AR, io):
    x2c_out, x3c_in = io["x2c_out"], io["x3c_in"]
    dtwbf, dtb, alog, dvec = io["dtwbf"], io["dtb"], io["alog"], io["dvec"]
    sidx_s = io["sidx_s"]
    PS4, PS2a, PS2b = io["PS4"], io["PS2a"], io["PS2b"]
    sv, ss, stt, sg = io["sv"], io["ss"], io["stt"], io["sg"]
    SEMS = (sv, ss, stt, sg)
    n_cast0 = io["n_cast0"]
    n_x2col = io["n_x2col"]
    DR = D + 160
    TC = 1024
    _barrier(kb, SEMS)
    AR.reset()
    dtinb = AR.take(T, BF16)
    brep = AR.take(16 * TC)
    crep = AR.take(16 * TC)
    dtw_s = AR.take(16 * 128, BF16)
    dtb_s = AR.take(16)
    a_s = AR.take(256)
    d1_s = AR.take(16)
    hcar = AR.take(256)
    htmp_t = [AR.take(1), AR.take(1)]
    _dt = AR.take(TC)
    dt_t = [_dt, _dt]
    sp_a = AR.take(TC)
    sp_e = [sp_a, sp_a]
    _u = AR.take(TC)
    u_t = [_u, _u]
    _xr = AR.take(TC)
    xr_t = [_xr, _xr]
    dA_t = [PS2b[:, 0:1024], PS4[:, 1024:2048]]
    dBx_t = [AR.take(TC), AR.take(TC)]
    h_t = [AR.take(TC), AR.take(TC)]
    ch_t = [AR.take(TC), AR.take(TC)]
    _y = AR.take(TC)
    y_t = [_y, _y]
    _z = AR.take(TC)
    z_t = [_z, _z]
    dti_f = _z
    ps_dt = [PS2a[:, 0:1024], PS4[:, 0:1024]]
    dq_sync = kb.dsem["sync"]
    dq_gp = kb.dsem["gpsimd"]

    kb.dma("sync", dtb_s, dtb)
    kb.dma("sync", a_s, alog.rearrange("(k p) n -> p k n", p=128))
    kb.dma("sync", d1_s, dvec)
    for kt in range(16):
        kb.dma("sync", dtw_s[:, kt * 128:(kt + 1) * 128], dtwbf.ap()[kt * 128:(kt + 1) * 128, 0:128],
               waits=n_cast0, transpose=True)
    for q in range(2):
        kb.dma("sync", dti_f, x2c_out.ap()[q * DR + D: q * DR + D + 128, :],
               waits=[(sg, n_x2col), sv])
        kb.emit("vector", lambda e, q=q: e.tensor_copy(dtinb[:, q * TME:(q + 1) * TME], dti_f),
                waits=["dma"])
    kb.emit("scalar", lambda e: e.activation(out=a_s, in_=a_s, func=AF.Exp), waits=["dma"])
    kb.emit("vector", lambda e: e.tensor_scalar_mul(a_s, a_s, -1.0), waits=[ss])
    kb.emit("scalar", lambda e: e.add(out=d1_s, in_=d1_s, add=1.0))
    kb.emit("vector", lambda e: e.memset(hcar, 0.0))

    NCH = T // TC
    for tc_i in range(NCH):
        tt0 = tc_i * TC
        q = tc_i
        for n_ in range(16):
            kb.dma("gpsimd", brep[:, n_ * TC:(n_ + 1) * TC],
                   x2c_out.ap()[q * DR + D + 128 + n_: q * DR + D + 129 + n_, :]
                   .broadcast_to([128, TC]), waits=[(sg, n_x2col), sg, sv])
            kb.dma("gpsimd", crep[:, n_ * TC:(n_ + 1) * TC],
                   x2c_out.ap()[q * DR + D + 144 + n_: q * DR + D + 145 + n_, :]
                   .broadcast_to([128, TC]))
        n_rep = kb.mark()

        v_scan = [0, 0]
        g_ch = [0, 0]
        v_yacc = [0, 0]
        v_z = [0, 0]
        st_z = [0, 0]
        s_dA = [0, 0]
        g_dBx = [0, 0]
        t_dt = [0, 0]
        end_dA = [0, 0]
        end_u = [0, 0]
        end_dBx = [0, 0]

        for kt in range(16):
            kp = kt % 2
            for half in range(2):
                kb.emit("tensor", lambda e, kt=kt, tt0=tt0, half=half, kp=kp: e.matmul(
                    ps_dt[kp][:, half * 512:(half + 1) * 512],
                    dtw_s[:, kt * 128:(kt + 1) * 128],
                    dtinb[:, tt0 + half * 512: tt0 + (half + 1) * 512], start=True, stop=True),
                    waits=[(ss, max(end_dA)), (sv, max(end_u)), "dma"], inc=stt)
            t_dt[kp] = stt.n
            kb.emit("scalar", lambda e, kt=kt, kp=kp: e.activation(
                out=dt_t[kp], in_=ps_dt[kp], func=AF.Identity, bias=dtb_s[:, kt:kt + 1]),
                waits=[(stt, t_dt[kp])])
            kb.emit("scalar", lambda e, kp=kp: e.activation(out=sp_a, in_=dt_t[kp], func=AF.Abs))
            kb.emit("scalar", lambda e: e.activation(out=sp_a, in_=sp_a, func=AF.Exp,
                                                     scale=-1.0))
            kb.emit("scalar", lambda e: e.activation(out=sp_a, in_=sp_a, func=AF.Ln,
                                                     bias=1.0))
            s_sp = ss.n
            kb.emit("vector", lambda e, kp=kp: e.tensor_scalar_max(dt_t[kp], dt_t[kp], 0.0),
                    waits=[(ss, s_sp)])
            kb.emit("vector", lambda e, kp=kp: e.tensor_tensor(out=dt_t[kp], in0=dt_t[kp],
                                                               in1=sp_e[kp], op=ALU.add))
            kb.idma(xr_t[kp], x2c_out.ap(), sidx_s[:, q * 16 + kt: q * 16 + kt + 1],
                    waits=[(sv, max(v_z))])
            n_xr = kb.mark()
            kb.emit("vector", lambda e, kp=kp: e.tensor_tensor(out=u_t[kp], in0=dt_t[kp],
                                                               in1=xr_t[kp], op=ALU.mult),
                    waits=[*n_xr, (sg, max(end_dBx))])
            v_u = sv.n
            end_u[kp] = v_u
            kb.emit("gpsimd", lambda e, kp=kp: e.memset(y_t[kp], 0.0), waits=[(sv, max(v_z))])
            g_y0 = sg.n

            for n_ in range(16):
                p = n_ % 2
                kb.emit("scalar", lambda e, kt=kt, n_=n_, kp=kp, p=p: e.activation(
                    out=dA_t[p], in_=dt_t[kp], func=AF.Exp,
                    scale=a_s[:, kt * 16 + n_: kt * 16 + n_ + 1]),
                    waits=[(sv, max(v_scan[p], v_u))])
                s_dA[p] = ss.n
                kb.emit("gpsimd", lambda e, n_=n_, kp=kp, p=p: e.tensor_tensor(
                    out=dBx_t[p], in0=u_t[kp], in1=brep[:, n_ * TC:(n_ + 1) * TC], op=ALU.mult),
                    waits=[*n_rep, (sv, max(v_scan[p], v_u))])
                g_dBx[p] = sg.n
                if tc_i == 0:
                    kb.emit("vector", lambda e, p=p: e.tensor_tensor_scan(
                        out=h_t[p], data0=dA_t[p], data1=dBx_t[p], initial=0.0,
                        op0=ALU.mult, op1=ALU.add),
                        waits=[(ss, s_dA[p]), (sg, max(g_dBx[p], g_ch[p]))])
                else:
                    kb.emit("vector", lambda e, kt=kt, n_=n_, p=p: e.tensor_tensor(
                        out=htmp_t[p], in0=dA_t[p][:, 0:1],
                        in1=hcar[:, kt * 16 + n_: kt * 16 + n_ + 1], op=ALU.mult),
                        waits=[(ss, s_dA[p]), (sg, max(g_dBx[p], g_ch[p]))])
                    v_htmp = sv.n
                    kb.emit("gpsimd", lambda e, p=p: e.tensor_tensor(
                        out=dBx_t[p][:, 0:1], in0=dBx_t[p][:, 0:1], in1=htmp_t[p], op=ALU.add),
                        waits=[(sv, v_htmp)])
                    g_fold = sg.n
                    kb.emit("vector", lambda e, p=p: e.tensor_tensor_scan(
                        out=h_t[p], data0=dA_t[p], data1=dBx_t[p], initial=0.0,
                        op0=ALU.mult, op1=ALU.add), waits=[(sg, g_fold)])
                v_scan[p] = sv.n
                if tc_i < NCH - 1:
                    kb.emit("vector", lambda e, kt=kt, n_=n_, p=p: e.tensor_copy(
                        hcar[:, kt * 16 + n_: kt * 16 + n_ + 1], h_t[p][:, TC - 1:TC]), flush=True)
                if p == 0:
                    kb.emit("gpsimd", lambda e, n_=n_, p=p: e.tensor_tensor(
                        out=ch_t[p], in0=h_t[p], in1=crep[:, n_ * TC:(n_ + 1) * TC], op=ALU.mult),
                        waits=[(sv, max(v_scan[p], v_yacc[p]))])
                    g_ch[p] = sg.n
                    kb.emit("vector", lambda e, p=p, kp=kp: e.tensor_tensor(
                        out=y_t[kp], in0=y_t[kp], in1=ch_t[p], op=ALU.add),
                        waits=[(sg, max(g_ch[p], g_y0))])
                    v_yacc[p] = sv.n
                else:
                    kb.emit("vector", lambda e, n_=n_, p=p: e.tensor_tensor(
                        out=ch_t[p], in0=h_t[p], in1=crep[:, n_ * TC:(n_ + 1) * TC], op=ALU.mult))
                    kb.emit("vector", lambda e, p=p, kp=kp: e.tensor_tensor(
                        out=y_t[kp], in0=y_t[kp], in1=ch_t[p], op=ALU.add),
                        waits=[(sg, g_y0)])
                    g_ch[p] = sg.n
                    v_yacc[p] = sv.n
            end_dA[kp] = ss.n
            end_dBx[kp] = sg.n
            sect = tc_i
            kb.emit("vector", lambda e, kt=kt, kp=kp: e.scalar_tensor_tensor(
                out=z_t[kp], in0=xr_t[kp], scalar=d1_s[:, kt:kt + 1], in1=y_t[kp],
                op0=ALU.mult, op1=ALU.add), waits=[(dq_sync, max(st_z))])
            v_z[kp] = sv.n
            kb.dma("sync", x3c_in.ap()[sect * 2048 + kt * 128: sect * 2048 + (kt + 1) * 128, :],
                   z_t[kp], waits=[(sv, v_z[kp])])
            st_z[kp] = dq_sync.n

    kb.emit("gpsimd", lambda e: e.collective_compute(
        "AllGather", ALU.bypass, replica_groups=[[0, 1], [2, 3], [4, 5], [6, 7]],
        ins=[x3c_in.ap().opt()], outs=[io["x3c_out"].ap().opt()]),
        waits=["dma", sv, ss, stt])
    io["n_x3col"] = io["sg"].n


def _mega_final(kb, nc, AR, io):
    x3c_out, cnt_in, cnt_out = io["x3c_out"], io["cnt_in"], io["cnt_out"]
    w1bf, gw1bf, loraA, loraB = io["w1bf"], io["gw1bf"], io["loraA"], io["loraB"]
    gb1, lng, lnb, gw2, gb2 = io["gb1"], io["lng"], io["lnb"], io["gw2"], io["gb2"]
    x3_d, zf_d, out_o = io["x3_d"], io["zf_d"], io["out_o"]
    xidx_s = io["xidx_s"]
    PS4, PS2a, PS2b = io["PS4"], io["PS2a"], io["PS2b"]
    identf_s, ones1_s, onescol_s, eps_s = io["identf_s"], io["ones1_s"], io["onescol_s"], io["eps_s"]
    ones8_s, repm_s = io["ones8_s"], io["repm_s"]
    sv, ss, stt, sg = io["sv"], io["ss"], io["stt"], io["sg"]
    SEMS = (sv, ss, stt, sg)
    n_cast0 = io["n_cast0"]
    n_x3col = io["n_x3col"]
    _barrier(kb, SEMS)
    AR.reset()
    off_x3b = AR.off
    x3b = AR.take(32 * TME, BF16)
    w1T = [AR.take(32 * 512, BF16), AR.take(32 * 512, BF16)]
    off_lA = AR.off
    loraA_s = AR.take(32 * 128, BF16)
    loraB_s = AR.take(D, BF16)
    gw1T = AR.take(32 * 32, BF16)
    gw2T_s = AR.take(8, BF16)[0:32, :]
    zt = AR.take(512)
    ztb = AR.take(512, BF16)
    sqs = AR.take(512)
    hf = AR.take(512)[0:32, :]
    hfb = AR.take(512, BF16)[0:32, :]
    gf = AR.take(512)[0:8, :]
    ht = AR.take(128)
    gt = AR.take(8)
    stats = AR.take(512)[0:1, :]
    stats2 = AR.take(512)[0:1, :]
    mu_r = AR.take(512)
    rs_r = AR.take(512)
    gb1_s = AR.take(1)[0:32, :]
    gb2_s = AR.take(1)[0:8, :]
    lng_r = AR.take(32)
    lnb_r = AR.take(32)
    gmu = AR.take(1)
    gvar = AR.take(1)
    grs = AR.take(1)
    hsq = AR.take(32)
    tm8 = AR.take(8)
    oh1 = AR.take(8)
    oh2 = AR.take(8)
    cum = AR.take(8)
    vmax = AR.take(1)
    v2max = AR.take(1)
    eg = AR.take(1)
    g1c = AR.take(1)
    g2c = AR.take(1)
    oh1my = AR.take(64)
    oh2my = AR.take(64)
    g1my = AR.take(8)
    g2my = AR.take(8)
    cnt = AR.take(2)[0:8, :]
    allw = AR.take(2)[0:8, :]
    allwT = AR.take(16)[0:2, :]
    al0_r = AR.take(8)
    al1_r = AR.take(8)
    wtk = AR.take(8)
    wtk2 = AR.take(8)
    wT_sb = AR.take(TME)[0:8, :]
    wrep = AR.take(TME)
    tmpm = AR.take(1024, BF16)
    x3res = [AR.take(512), AR.take(512)]
    zft = [AR.take(512), AR.take(512)]
    sqs2 = [AR.take(512), AR.take(512)]
    zftT2 = [AR.take(512), AR.take(512)]
    fstat = AR.at(off_lA, TME)[0:1, :]
    fstat2 = AR.at(off_lA + TME, TME)[0:1, :]
    fmuT = AR.take(8)
    frsT = AR.take(8)
    outt = AR.at(off_x3b, D)
    ps_a = PS2a[:, 0:512]
    ps_m = [PS2a[:, 0:512], PS2b[:, 0:512]]
    ps_tr = [PS2a[:, 512:1024], PS2b[:, 512:1024]]
    ps_h = PS2a[0:32, 512:1024]
    ps_g = PS2a[0:32, 512:1024]
    ps_t = PS2b[:, 0:512]
    ps_cnt = PS2b[0:8, 512:514]
    ps_st = PS4[0:1, 0:1024]
    ps_st2 = PS4[0:1, 1024:2048]
    NT = NCORES * TME
    CAP = float(int(3.0 * NT / 8))

    for dt_ in range(32):
        kb.dma("gpsimd", loraA_s[:, dt_ * 128:(dt_ + 1) * 128].rearrange("p (e r) -> p e r", e=8),
               loraA.rearrange("(e d) r -> d e r", e=8)[dt_ * 128:(dt_ + 1) * 128, :, :])
    kb.dma("gpsimd", loraB_s, loraB)
    kb.dma("gpsimd", gw2T_s, gw2)
    kb.dma("gpsimd", lng_r, lng.broadcast_to([128, 32]))
    kb.dma("gpsimd", lnb_r, lnb.broadcast_to([128, 32]))
    kb.dma("sync", gb1_s, gb1)
    kb.dma("sync", gb2_s, gb2)
    for dt_ in range(32):
        kb.dma("sync", gw1T[:, dt_ * 32:(dt_ + 1) * 32],
               gw1bf.ap()[:, dt_ * 128:(dt_ + 1) * 128], waits=n_cast0, transpose=True)
    n_small = kb.mark()

    for tch in range(2):
        for dt_ in range(32):
            kb.idma(zt, x3c_out.ap(), xidx_s[:, dt_:dt_ + 1],
                    waits=[(sg, n_x3col), ss, stt, sv], element_offset=tch * 512)
            kb.emit("scalar", lambda e: e.square(out=sqs, in_=zt), waits=["dma", stt])

            def mst(e, dt_=dt_):
                e.matmul(ps_st[:, 0:512], onescol_s[:], zt, start=(dt_ == 0), stop=(dt_ == 31),
                         skip_group_check=True)
                return e.matmul(ps_st2[:, 0:512], onescol_s[:], sqs, start=(dt_ == 0),
                                stop=(dt_ == 31), skip_group_check=True)
            kb.emit("tensor", mst, waits=[ss, *n_small], inc=stt)
        kb.emit("vector", lambda e: e.tensor_scalar_mul(stats, ps_st[:, 0:512], 1.0 / D), waits=[stt])
        kb.emit("vector", lambda e: e.tensor_scalar_mul(stats2, ps_st2[:, 0:512], 1.0 / D))
        kb.emit("vector", lambda e: e.tensor_tensor(out=sqs[0:1, :], in0=stats, in1=stats, op=ALU.mult))
        kb.emit("vector", lambda e: e.tensor_tensor(out=stats2, in0=stats2, in1=sqs[0:1, :],
                                                    op=ALU.subtract))
        kb.emit("scalar", lambda e: e.activation(out=stats2, in_=stats2, func=AF.Ln,
                                                 bias=eps_s[0:1, 0:1]), waits=[sv])
        kb.emit("scalar", lambda e: e.activation(out=stats2, in_=stats2, func=AF.Exp, scale=-0.5))
        kb.emit("vector", lambda e: e.tensor_scalar_mul(stats, stats, -1.0), waits=[ss])
        kb.emit("tensor", lambda e: e.matmul(ps_a, ones1_s[:], stats[0:1, :], start=True, stop=True),
                waits=[sv, ss], inc=stt)
        kb.emit("scalar", lambda e: e.copy(out=mu_r, in_=ps_a), waits=[stt])
        kb.emit("tensor", lambda e: e.matmul(ps_a, ones1_s[:], stats2[0:1, :], start=True, stop=True),
                waits=[ss], inc=stt)
        kb.emit("scalar", lambda e: e.copy(out=rs_r, in_=ps_a), waits=[stt])

        for dt_ in range(32):
            kb.idma(zt, x3c_out.ap(), xidx_s[:, dt_:dt_ + 1],
                    waits=[ss, stt, sv], element_offset=tch * 512)
            kb.emit("vector", lambda e: e.tensor_tensor(out=zt, in0=zt, in1=mu_r, op=ALU.add),
                    waits=["dma", ss])
            kb.emit("vector", lambda e: e.tensor_tensor(out=zt, in0=zt, in1=rs_r, op=ALU.mult))
            kb.emit("vector", lambda e, dt_=dt_, tch=tch: e.tensor_copy(
                x3b[:, dt_ * TME + tch * 512: dt_ * TME + (tch + 1) * 512], zt))
            kb.dma("scalar", x3_d.ap()[dt_ * 128:(dt_ + 1) * 128, tch * 512:(tch + 1) * 512], zt,
                   waits=[sv])
            kb.emit("vector", lambda e: e.tensor_copy(ztb, zt))

            def mh(e, dt_=dt_):
                return e.matmul(ps_h[:, 0:512], gw1T[:, dt_ * 32:(dt_ + 1) * 32], ztb,
                                start=(dt_ == 0), stop=(dt_ == 31))
            kb.emit("tensor", mh, waits=[sv, *n_small], inc=stt)

        kb.flush_all = True
        kb.emit("scalar", lambda e: e.activation(out=hf, in_=ps_h[:, 0:512], func=AF.Identity,
                                                 bias=gb1_s[:, 0:1]), waits=[stt])

        def trh(e):
            for tt in range(4):
                i = e.transpose(ps_t[:, tt * 32:(tt + 1) * 32], hf[0:32, tt * 128:(tt + 1) * 128],
                                identf_s[0:32, 0:32])
            return i
        kb.emit("tensor", trh, waits=[ss, sv], inc=stt)
        kb.emit("vector", lambda e: e.tensor_copy(ht, ps_t[:, 0:128]), waits=[stt])
        for tt in range(4):
            htt = ht[:, tt * 32:(tt + 1) * 32]
            kb.emit("vector", lambda e, htt=htt: e.reduce_sum(gmu, htt, axis=AX))
            kb.emit("vector", lambda e: e.tensor_scalar_mul(gmu, gmu, 1.0 / 32.0))
            kb.emit("vector", lambda e, htt=htt: e.tensor_scalar(
                out=htt, in0=htt, scalar1=gmu[:, 0:1], scalar2=None, op0=ALU.subtract), flush=True)
            kb.emit("scalar", lambda e, htt=htt: e.square(out=hsq, in_=htt), waits=[sv])
            kb.emit("vector", lambda e: e.reduce_sum(gvar, hsq, axis=AX), waits=[ss])
            kb.emit("vector", lambda e: e.tensor_scalar_mul(gvar, gvar, 1.0 / 32.0))
            kb.emit("scalar", lambda e: e.activation(out=gvar, in_=gvar, func=AF.Ln,
                                                     bias=eps_s[:, 0:1]), waits=[sv])
            kb.emit("scalar", lambda e: e.activation(out=grs, in_=gvar, func=AF.Exp, scale=-0.5))
            kb.emit("vector", lambda e, htt=htt: e.tensor_scalar(
                out=htt, in0=htt, scalar1=grs[:, 0:1], scalar2=None, op0=ALU.mult), waits=[ss])
            kb.emit("vector", lambda e, htt=htt: e.tensor_tensor(out=htt, in0=htt, in1=lng_r, op=ALU.mult),
                    waits=["dma"])
            kb.emit("vector", lambda e, htt=htt: e.tensor_tensor(out=htt, in0=htt, in1=lnb_r, op=ALU.add))
            kb.emit("vector", lambda e, htt=htt: e.tensor_scalar_max(htt, htt, 0.0))

        def trh2(e):
            for tt in range(4):
                i = e.transpose(ps_h[:, 512:1024][0:32, tt * 128:(tt + 1) * 128] if False else
                                ps_g[:, 0:512][0:32, tt * 128:(tt + 1) * 128],
                                ht[:, tt * 32:(tt + 1) * 32], identf_s[:])
            return i
        kb.emit("tensor", trh2, waits=[sv, ss], inc=stt)
        kb.emit("vector", lambda e: e.tensor_copy(hfb, ps_g[:, 0:512][0:32, :]), waits=[stt])
        kb.emit("tensor", lambda e: e.matmul(ps_g[0:8, 0:512], gw2T_s, hfb, start=True, stop=True),
                waits=[sv, ss], inc=stt)
        kb.emit("scalar", lambda e: e.activation(out=gf, in_=ps_g[0:8, 0:512], func=AF.Identity,
                                                 bias=gb2_s[:, 0:1]), waits=[stt])
        for tt in range(4):
            kb.emit("tensor", lambda e, tt=tt: e.transpose(
                ps_t[:, 0:8], gf[0:8, tt * 128:(tt + 1) * 128], identf_s[0:8, 0:8]),
                waits=[ss, sv], inc=stt)
            kb.emit("vector", lambda e: e.tensor_copy(gt, ps_t[:, 0:8]), waits=[stt])
            kb.emit("vector", lambda e: e.reduce_max(vmax, gt, axis=AX))
            kb.emit("vector", lambda e: e.tensor_scalar(
                out=oh1, in0=gt, scalar1=vmax[:, 0:1], scalar2=None, op0=ALU.is_ge), flush=True)
            kb.emit("vector", lambda e: e.tensor_tensor_scan(
                out=cum, data0=ones8_s[:], data1=oh1, initial=0.0, op0=ALU.mult, op1=ALU.add))
            kb.emit("vector", lambda e: e.tensor_scalar(
                out=cum, in0=cum, scalar1=1.0, scalar2=None, op0=ALU.is_le))
            kb.emit("vector", lambda e: e.tensor_tensor(out=oh1, in0=oh1, in1=cum, op=ALU.mult))
            kb.emit("vector", lambda e: e.scalar_tensor_tensor(
                out=tm8, in0=oh1, scalar=-1.0e30, in1=gt, op0=ALU.mult, op1=ALU.add))
            kb.emit("vector", lambda e: e.reduce_max(v2max, tm8, axis=AX))
            kb.emit("vector", lambda e: e.tensor_scalar(
                out=oh2, in0=tm8, scalar1=v2max[:, 0:1], scalar2=None, op0=ALU.is_ge), flush=True)
            kb.emit("vector", lambda e: e.tensor_tensor_scan(
                out=cum, data0=ones8_s[:], data1=oh2, initial=0.0, op0=ALU.mult, op1=ALU.add))
            kb.emit("vector", lambda e: e.tensor_scalar(
                out=cum, in0=cum, scalar1=1.0, scalar2=None, op0=ALU.is_le))
            kb.emit("vector", lambda e: e.tensor_tensor(out=oh2, in0=oh2, in1=cum, op=ALU.mult))
            first = (tch == 0 and tt == 0)
            last = (tch == 1 and tt == 3)

            def mcnt(e, first=first, last=last):
                e.matmul(ps_cnt[:, 0:1], oh1, onescol_s[:], start=first, stop=last,
                         skip_group_check=True)
                return e.matmul(ps_cnt[:, 1:2], oh2, onescol_s[:], start=first, stop=last,
                                skip_group_check=True)
            kb.emit("tensor", mcnt, waits=[sv], inc=stt)
            myt = tch * 4 + tt
            kb.emit("vector", lambda e, myt=myt: e.tensor_copy(oh1my[:, myt * 8:(myt + 1) * 8], oh1),
                    waits=[stt])
            kb.emit("vector", lambda e, myt=myt: e.tensor_copy(oh2my[:, myt * 8:(myt + 1) * 8], oh2))
            kb.emit("vector", lambda e: e.tensor_tensor(out=eg, in0=v2max, in1=vmax, op=ALU.subtract))
            kb.emit("scalar", lambda e: e.activation(out=eg, in_=eg, func=AF.Exp), waits=[sv])
            kb.emit("vector", lambda e: e.tensor_scalar_add(g2c, eg, 1.0), waits=[ss])
            kb.emit("vector", lambda e: e.reciprocal(g1c, g2c))
            kb.emit("vector", lambda e, myt=myt: e.tensor_copy(g1my[:, myt:myt + 1], g1c))
            kb.emit("vector", lambda e, myt=myt: e.tensor_tensor(
                out=g2my[:, myt:myt + 1], in0=eg, in1=g1c, op=ALU.mult))
        kb.flush_all = False

    kb.flush_all = True
    kb.emit("vector", lambda e: e.tensor_copy(cnt, ps_cnt), waits=[stt])
    kb.dma("sync", cnt_in.ap(), cnt, waits=[sv])
    kb.emit("gpsimd", lambda e: e.collective_compute(
        "AllReduce", ALU.add, replica_groups=[[0, 1, 2, 3, 4, 5, 6, 7]],
        ins=[cnt_in.ap().opt()], outs=[cnt_out.ap().opt()]), waits=["dma"])
    n_ccnt = sg.n
    kb.dma("sync", cnt, cnt_out.ap(), waits=[(sg, n_ccnt), sv])
    kb.emit("vector", lambda e: e.tensor_scalar(
        out=allw, in0=cnt, scalar1=CAP, scalar2=None, op0=ALU.is_le), waits=["dma"])
    kb.emit("tensor", lambda e: e.transpose(ps_t[0:2, 0:8], allw, identf_s[0:8, 0:8]),
            waits=[sv], inc=stt)
    kb.emit("vector", lambda e: e.tensor_copy(allwT[0:2, 0:8], ps_t[0:2, 0:8]), waits=[stt])
    kb.dma("sync", io["allw_d"].ap(), allwT[0:2, 0:8], waits=[sv])
    kb.dma("sync", allwT[0:1, 0:16].rearrange("p (a c) -> p a c", a=2), io["allw_d"].ap().unsqueeze(0),
           waits=["dma"])
    kb.emit("tensor", lambda e: e.matmul(ps_a[:, 0:8], ones1_s[:], allwT[0:1, 0:8],
                                         start=True, stop=True), waits=["dma", sv], inc=stt)
    kb.emit("scalar", lambda e: e.copy(out=al0_r, in_=ps_a[:, 0:8]), waits=[stt])
    kb.emit("tensor", lambda e: e.matmul(ps_a[:, 0:8], ones1_s[:], allwT[0:1, 8:16],
                                         start=True, stop=True), waits=[ss], inc=stt)
    kb.emit("scalar", lambda e: e.copy(out=al1_r, in_=ps_a[:, 0:8]), waits=[stt])

    for myt in range(8):
        kb.emit("vector", lambda e, myt=myt: e.tensor_scalar(
            out=wtk, in0=oh1my[:, myt * 8:(myt + 1) * 8], scalar1=g1my[:, myt:myt + 1],
            scalar2=None, op0=ALU.mult), waits=[ss], flush=True)
        kb.emit("vector", lambda e: e.tensor_tensor(out=wtk, in0=wtk, in1=al0_r, op=ALU.mult))
        kb.emit("vector", lambda e, myt=myt: e.tensor_scalar(
            out=wtk2, in0=oh2my[:, myt * 8:(myt + 1) * 8], scalar1=g2my[:, myt:myt + 1],
            scalar2=None, op0=ALU.mult), flush=True)
        kb.emit("vector", lambda e: e.tensor_tensor(out=wtk2, in0=wtk2, in1=al1_r, op=ALU.mult))
        kb.emit("vector", lambda e: e.tensor_tensor(out=wtk, in0=wtk, in1=wtk2, op=ALU.add))
        kb.emit("tensor", lambda e: e.transpose(ps_g[0:8, 0:128], wtk, identf_s[:]),
                waits=[sv, ss], inc=stt)
        kb.emit("scalar", lambda e, myt=myt: e.copy(out=wT_sb[:, myt * 128:(myt + 1) * 128],
                                                    in_=ps_g[0:8, 0:128]), waits=[stt])

    for tc in range(2):
        kb.emit("tensor", lambda e, tc=tc: e.matmul(ps_t, repm_s[:], wT_sb[:, tc * 512:(tc + 1) * 512],
                                                    start=True, stop=True), waits=[ss, sv], inc=stt)
        kb.emit("scalar", lambda e, tc=tc: e.copy(out=wrep[:, tc * 512:(tc + 1) * 512], in_=ps_t),
                waits=[stt])
    kb.flush_all = False

    for tc in range(2):
        def mtmp(e, tc=tc):
            for dt_ in range(32):
                i = e.matmul(ps_a, loraA_s[:, dt_ * 128:(dt_ + 1) * 128],
                             x3b[:, dt_ * TME + tc * 512: dt_ * TME + (tc + 1) * 512],
                             start=(dt_ == 0), stop=(dt_ == 31))
            return i
        kb.emit("tensor", mtmp, waits=["dma", sv, ss], inc=stt)
        kb.emit("vector", lambda e, tc=tc: e.tensor_tensor(
            out=tmpm[:, tc * 512:(tc + 1) * 512], in0=ps_a, in1=wrep[:, tc * 512:(tc + 1) * 512],
            op=ALU.mult), waits=[stt, ss])
    n_tmpm = sv.n

    dq_sync2 = kb.dsem["sync"]
    dq_sc2 = kb.dsem["scalar"]
    n_wload = [0, 0]
    n_mmw = [0, 0]
    n_mm = [0, 0]
    v_evac = [n_tmpm, n_tmpm]
    v_zft = [0, 0]
    v_zftT = [0, 0]
    n_sq_p = [0, 0]
    n_x3res = [0, 0]
    st_zf = [0, 0]
    for dt_ in range(32):
        kb.dma("sync", w1T[0][:, dt_ * 512:(dt_ + 1) * 512],
               w1bf.ap()[0:512, dt_ * 128:(dt_ + 1) * 128],
               waits=[*n_cast0], transpose=True)
    n_wload[0] = dq_sync2.n
    for og in range(8):
        wp = og % 2
        if og + 1 < 8:
            for dt_ in range(32):
                kb.dma("sync", w1T[1 - wp][:, dt_ * 512:(dt_ + 1) * 512],
                       w1bf.ap()[(og + 1) * 512:(og + 2) * 512, dt_ * 128:(dt_ + 1) * 128],
                       waits=[*n_cast0, (stt, n_mmw[1 - wp])], transpose=True)
            n_wload[1 - wp] = dq_sync2.n
        for osub in range(4):
            ot = og * 4 + osub
            for tc in range(2):
                it = ot * 2 + tc
                ip = it % 2
                kb.dma("scalar", x3res[ip],
                       x3_d.ap()[ot * 128:(ot + 1) * 128, tc * 512:(tc + 1) * 512],
                       waits=[(sv, v_zft[ip])])
                n_x3res[ip] = dq_sc2.n

                def mmain(e, osub=osub, tc=tc, ot=ot, ip=ip, wp=wp):
                    for dt_ in range(32):
                        e.matmul(ps_m[ip],
                                 w1T[wp][:, dt_ * 512 + osub * 128: dt_ * 512 + (osub + 1) * 128],
                                 x3b[:, dt_ * TME + tc * 512: dt_ * TME + (tc + 1) * 512],
                                 start=(dt_ == 0), stop=False)
                    return e.matmul(ps_m[ip], loraB_s[:, ot * 128:(ot + 1) * 128],
                                    tmpm[:, tc * 512:(tc + 1) * 512], start=False, stop=True)
                kb.emit("tensor", mmain,
                        waits=[(dq_sync2, n_wload[wp]), (sv, v_evac[ip]), *n_small], inc=stt)
                n_mm[ip] = stt.n
                n_mmw[wp] = stt.n
                kb.emit("vector", lambda e, ip=ip: e.tensor_tensor(
                    out=zft[ip], in0=ps_m[ip], in1=x3res[ip], op=ALU.add),
                    waits=[(stt, n_mm[ip]), (dq_sc2, n_x3res[ip]), (ss, n_sq_p[ip])])
                v_zft[ip] = sv.n
                v_evac[ip] = sv.n
                kb.emit("scalar", lambda e, ip=ip: e.square(out=sqs2[ip], in_=zft[ip]),
                        waits=[(sv, v_zft[ip])])
                n_sq = ss.n
                n_sq_p[ip] = n_sq
                first = (ot == 0)
                last = (ot == 31)

                def mfst(e, first=first, last=last, tc=tc, ip=ip):
                    e.matmul(ps_st[:, tc * 512:(tc + 1) * 512], onescol_s[:], zft[ip],
                             start=first, stop=last, skip_group_check=True)
                    return e.matmul(ps_st2[:, tc * 512:(tc + 1) * 512], onescol_s[:], sqs2[ip],
                                    start=first, stop=last, skip_group_check=True)
                kb.emit("tensor", mfst, waits=[(ss, n_sq)], inc=stt)

                def mtr(e, ip=ip):
                    for tt in range(4):
                        i = e.matmul(ps_tr[ip][:, tt * 128:(tt + 1) * 128],
                                     zft[ip][:, tt * 128:(tt + 1) * 128], identf_s[:],
                                     start=True, stop=True, is_transpose=True)
                    return i
                kb.emit("tensor", mtr, waits=[(sv, max(v_zft[ip], v_zftT[ip]))], inc=stt)
                n_mtr = stt.n
                kb.emit("vector", lambda e, ip=ip: e.tensor_copy(zftT2[ip], ps_tr[ip]),
                        waits=[(stt, n_mtr), (dq_sync2, st_zf[ip])])
                v_zftT[ip] = sv.n
                kb.dma("sync", zf_d.ap()[tc * 512:(tc + 1) * 512, ot * 128:(ot + 1) * 128]
                       .rearrange("(a p) f -> p a f", p=128),
                       zftT2[ip].rearrange("p (a f) -> p a f", a=4),
                       waits=[(sv, v_zftT[ip])])
                st_zf[ip] = dq_sync2.n
    n_zf = kb.mark()

    kb.emit("vector", lambda e: e.tensor_scalar_mul(fstat, ps_st[:, 0:TME], 1.0 / D), waits=[stt])
    kb.emit("vector", lambda e: e.tensor_scalar_mul(fstat2, ps_st2[:, 0:TME], 1.0 / D))
    kb.emit("vector", lambda e: e.tensor_tensor(out=sqs[0:1, :], in0=fstat[0:1, 0:512],
                                                in1=fstat[0:1, 0:512], op=ALU.mult))
    kb.emit("vector", lambda e: e.tensor_tensor(out=fstat2[0:1, 0:512], in0=fstat2[0:1, 0:512],
                                                in1=sqs[0:1, :], op=ALU.subtract))
    kb.emit("vector", lambda e: e.tensor_tensor(out=sqs[0:1, :], in0=fstat[0:1, 512:1024],
                                                in1=fstat[0:1, 512:1024], op=ALU.mult))
    kb.emit("vector", lambda e: e.tensor_tensor(out=fstat2[0:1, 512:1024], in0=fstat2[0:1, 512:1024],
                                                in1=sqs[0:1, :], op=ALU.subtract))
    kb.emit("scalar", lambda e: e.activation(out=fstat2, in_=fstat2, func=AF.Ln, bias=eps_s[0:1, 0:1]),
            waits=[sv])
    kb.emit("scalar", lambda e: e.activation(out=fstat2, in_=fstat2, func=AF.Exp, scale=-0.5))
    kb.emit("vector", lambda e: e.tensor_scalar_mul(fstat, fstat, -1.0), waits=[ss])
    for tt in range(8):
        def mtrs(e, tt=tt):
            e.matmul(ps_t[:, 0:1], fstat[0:1, tt * 128:(tt + 1) * 128], identf_s[0:1, 0:1],
                     start=True, stop=True, is_transpose=True)
            return e.matmul(ps_t[:, 1:2], fstat2[0:1, tt * 128:(tt + 1) * 128], identf_s[0:1, 0:1],
                            start=True, stop=True, is_transpose=True)
        kb.emit("tensor", mtrs, waits=[sv, ss], inc=stt)
        kb.emit("scalar", lambda e, tt=tt: e.copy(out=fmuT[:, tt:tt + 1], in_=ps_t[:, 0:1]),
                waits=[stt])
        kb.emit("scalar", lambda e, tt=tt: e.copy(out=frsT[:, tt:tt + 1], in_=ps_t[:, 1:2]))

    io["dbg_wrep"] = wrep
    io["dbg_tmpm"] = tmpm
    io["dbg_wT"] = wT_sb
    io["dbg_al0"] = al0_r
    io["dbg_al1"] = al1_r
    io["dbg_g1my"] = g1my
    io["dbg_o1my"] = oh1my
    io["dbg_cnt"] = cnt
    io["dbg_awt"] = allwT
    for tt in range(8):
        kb.dma("sync", outt, zf_d.ap()[tt * 128:(tt + 1) * 128, :], waits=[*n_zf, ss, sv, "dma"])
        kb.emit("vector", lambda e, tt=tt: e.tensor_scalar(
            out=outt, in0=outt, scalar1=fmuT[:, tt:tt + 1], scalar2=None, op0=ALU.add),
            waits=["dma", ss])
        kb.emit("vector", lambda e, tt=tt: e.tensor_scalar(
            out=outt, in0=outt, scalar1=frsT[:, tt:tt + 1], scalar2=None, op0=ALU.mult))
        kb.dma("sync", out_o[tt * 128:(tt + 1) * 128, :], outt, waits=[sv])


def build_mega(debug=False):
    nc = bass.Bass("TRN2", target_bir_lowering=False, debug=False)
    I = lambda n, sh, dt: nc.dram_tensor(n, sh, dt, kind="ExternalInput").ap()
    xb = I("xb", [T, D], F32)
    xmy = I("xmy", [TME, D], F32)
    wq = I("wq", [2048, D], F32)
    wqb = I("wqb", [128, 16], F32)
    wk = I("wk", [128, D], F32)
    wkb = I("wkb", [128, 1], F32)
    wv = I("wv", [128, D], F32)
    wvb = I("wvb", [128, 1], F32)
    wo = I("wo", [D, D], F32)
    wob = I("wob", [128, 32], F32)
    qw = I("qw", [512, 512], I32)
    qz = I("qz", [32, 64], I32)
    sc = I("sc", [32, 512], F32)
    gidx = I("gidx", [D, 1], I32)
    fbias = I("fbias", [128, 32], F32)
    xproj = I("xproj", [160, D], F32)
    dtw = I("dtw", [2048, 128], F32)
    dtb = I("dtb", [128, 16], F32)
    alog = I("alog", [2048, 16], F32)
    dvec = I("dvec", [128, 16], F32)
    w1 = I("w1", [D, D], F32)
    loraA = I("loraA", [8 * D, 16], F32)
    loraB = I("loraB", [128, D], F32)
    gw1 = I("gw1", [32, D], F32)
    gb1 = I("gb1", [32, 1], F32)
    lng = I("lng", [1, 32], F32)
    lnb = I("lnb", [1, 32], F32)
    gw2 = I("gw2", [32, 8], F32)
    gb2 = I("gb2", [8, 1], F32)
    xidx = I("xidx", [128, 32], I32)
    sidx = I("sidx", [128, 32], I32)
    out_o = nc.dram_tensor("out", [TME, D], F32, kind="ExternalOutput").ap()

    Dm = lambda n, sh, dt: nc.dram_tensor(n, sh, dt)
    xbf = Dm("xbf", [T, D], BF16)
    wqbf = Dm("wqbf", [2048, D], BF16)
    wkbf = Dm("wkbf", [128, D], BF16)
    wvbf = Dm("wvbf", [128, D], BF16)
    wobf = Dm("wobf", [D, D], BF16)
    w1bf = Dm("w1bf", [D, D], BF16)
    gw1bf = Dm("gw1bf", [32, D], BF16)
    dtwbf = Dm("dtwbf", [2048, 128], BF16)
    xpjbf = Dm("xpjbf", [160, D], BF16)
    allw_d = Dm("allw_d", [2, 8], F32)
    x1c_in = Dm("x1c_in", [2 * 2048, TME], BF16)
    x1c_out = Dm("x1c_out", [2 * 4096, TME], BF16)
    wsl_in = Dm("wsl_in", [D, 512], BF16)
    wsl_out = Dm("wsl_out", [8 * D, 512], BF16)
    z_d = Dm("z_d", [D, TME], F32)
    x1_d = Dm("x1_d", [D, TME], F32)
    x2c_in = Dm("x2c_in", [D + 160, TME], F32)
    x2c_out = Dm("x2c_out", [2 * (D + 160), TME], F32)
    x3c_in = Dm("x3c_in", [2 * 2048, TME], F32)
    x3c_out = Dm("x3c_out", [2 * 4096, TME], F32)
    cnt_in = Dm("cnt_in", [8, 2], F32)
    cnt_out = Dm("cnt_out", [8, 2], F32)
    x3_d = Dm("x3_d", [D, TME], F32)
    zf_d = Dm("zf_d", [TME, D], F32)

    cs = _consts(nc)
    repmat = np.zeros((8, 128), np.float32)
    for e_ in range(8):
        repmat[e_, e_ * 16:(e_ + 1) * 16] = 1.0 / 16.0
    rep_c = nc.inline_tensor(repmat, name="c_repmat").ap()
    ones8_c = nc.inline_tensor(np.ones((128, 8), np.float32), name="c_ones8").ap()
    shifts_c = nc.inline_tensor((4 * (np.arange(128, dtype=np.int32) % 8)).reshape(128, 1),
                                name="c_shifts").ap()
    iota32_c = nc.inline_tensor(np.broadcast_to(np.arange(32, dtype=np.float32), (128, 32)).copy(),
                                name="c_iota32").ap()

    kb = KB(nc)
    sv, ss, stt, sg = (kb.sem(x) for x in ("sv", "ss", "stt", "sg"))
    kb.set_self("vector", sv)
    kb.set_self("scalar", ss)
    kb.set_self("gpsimd", sg)
    SEMS = (sv, ss, stt, sg)

    es = ExitStack()
    with es:
        E = lambda n, sh, dt: es.enter_context(nc.sbuf_tensor(n, sh, dt))
        identf_s = E("identf_s", [128, 128], F32)
        identb_s = E("identb_s", [128, 128], BF16)
        mask_s = E("mask_s", [128, 128], F32)
        ones1_s = E("ones1_s", [1, 128], F32)
        onescol_s = E("onescol_s", [128, 1], F32)
        eps_s = E("eps_s", [128, 1], F32)
        iota_s = E("iota_s", [128, 32], F32)
        shf_s = E("shf_s", [128, 1], I32)
        ones8_s = E("ones8_s", [128, 8], F32)
        repm_s = E("repm_s", [8, 128], F32)
        xidx_s = E("xidx_s", [128, 32], I32)
        sidx_s = E("sidx_s", [128, 32], I32)
        PS4 = es.enter_context(nc.psum_tensor("PS4", [128, 2048], F32))
        PS2a = es.enter_context(nc.psum_tensor("PS2a", [128, 1024], F32))
        PS2b = es.enter_context(nc.psum_tensor("PS2b", [128, 1024], F32))
        AR = Ar(nc, es, "AR", 49664)

        kb.dma("sync", identf_s[:], cs["identf"])
        kb.dma("sync", identb_s[:], cs["identb"])
        kb.dma("sync", mask_s[:], cs["mask"])
        kb.dma("sync", ones1_s[:], cs["ones1"])
        kb.dma("sync", onescol_s[:], cs["onescol"])
        kb.dma("sync", eps_s[:], cs["eps"])
        kb.dma("sync", iota_s[:], iota32_c)
        kb.dma("sync", shf_s[:], shifts_c)
        kb.dma("sync", ones8_s[:], ones8_c)
        kb.dma("sync", repm_s[:], rep_c)
        kb.dma("sync", xidx_s[:], xidx)
        kb.dma("sync", sidx_s[:], sidx)

        for i in range(4):
            kb.dma("gpsimd", xbf.ap()[i * 512:(i + 1) * 512, :], xb[i * 512:(i + 1) * 512, :])
            kb.dma("gpsimd", wqbf.ap()[i * 512:(i + 1) * 512, :], wq[i * 512:(i + 1) * 512, :])
        kb.dma("gpsimd", wkbf.ap(), wk)
        kb.dma("gpsimd", wvbf.ap(), wv)
        for i in range(8):
            kb.dma("gpsimd", wobf.ap()[i * 512:(i + 1) * 512, :], wo[i * 512:(i + 1) * 512, :])
            kb.dma("gpsimd", w1bf.ap()[i * 512:(i + 1) * 512, :], w1[i * 512:(i + 1) * 512, :])
        kb.dma("gpsimd", gw1bf.ap(), gw1)
        kb.dma("gpsimd", dtwbf.ap(), dtw)
        kb.dma("gpsimd", xpjbf.ap(), io_xproj if False else xproj)
        n_cast0 = kb.mark()

        io = dict(locals())
        _mega_dequant(kb, nc, AR, io)
        _mega_attn(kb, nc, AR, io)
        _mega_wo(kb, nc, AR, io)
        _mega_ffn(kb, nc, AR, io)
        _mega_ssm(kb, nc, AR, io)
        _mega_final(kb, nc, AR, io)

        if debug:
            for nm2, ap2, sh2, dt2 in [("d_wrep", io["dbg_wrep"], [128, TME], F32),
                                       ("d_tmpm", io["dbg_tmpm"], [128, TME], BF16),
                                       ("d_wT", io["dbg_wT"], [8, TME], F32),
                                       ("d_al0", io["dbg_al0"], [128, 8], F32),
                                       ("d_al1", io["dbg_al1"], [128, 8], F32),
                                       ("d_g1my", io["dbg_g1my"], [128, 8], F32),
                                       ("d_o1my", io["dbg_o1my"], [128, 64], F32),
                                       ("d_cnt", io["dbg_cnt"], [8, 2], F32),
                                       ("d_awt", io["dbg_awt"], [2, 16], F32)]:
                od2 = nc.dram_tensor(nm2, sh2, dt2, kind="ExternalOutput").ap()
                kb.dma("sync", od2, ap2, waits=["dma", sv, ss, stt, sg])
            dbgs = [("d_x1c", x1c_in, [2 * 2048, TME], BF16),
                    ("d_wsl", wsl_in, [D, 512], BF16),
                    ("d_x1", x1_d, [D, TME], F32),
                    ("d_x2c", x2c_in, [D + 160, TME], F32),
                    ("d_x3c", x3c_in, [2 * 2048, TME], F32),
                    ("d_x3", x3_d, [D, TME], F32),
                    ("d_zf", zf_d, [TME, D], F32)]
            for nm, src_t, sh, dt in dbgs:
                od = nc.dram_tensor(nm, sh, dt, kind="ExternalOutput").ap()
                kb.dma("sync", od, src_t.ap(), waits=["dma", sv, ss, stt, sg])
        kb.emit("sync", lambda e: e.nop(), waits=["dma", sv, ss, stt, sg])
        kb.build(es)
    return nc


def _get(name, builder):
    if name not in _NC_CACHE:
        _NC_CACHE[name] = builder()
    return _NC_CACHE[name]


def _run(name, builder, in_maps):
    nc = _get(name, builder)
    res = run_bass_kernel_spmd(nc, in_maps, core_ids=list(range(NCORES)))
    return res.results


def kernel_v0(L0_attn_x, L0_attn_wq_w, L0_attn_wq_b, L0_attn_wk_w, L0_attn_wk_b,
           L0_attn_wv_w, L0_attn_wv_b, L0_attn_wo_w, L0_attn_wo_b,
           L0_ffn_qweight, L0_ffn_qzeros, L0_ffn_scales, L0_ffn_g_idx, L0_ffn_bias,
           L1_attn_A_log, L1_attn_D, L1_attn_x_proj_w, L1_attn_dt_proj_w, L1_attn_dt_proj_b,
           L1_ffn_weight, L1_ffn_lora_A, L1_ffn_lora_B, L1_ffn_gw1, L1_ffn_gb1,
           L1_ffn_ln_gamma, L1_ffn_ln_beta, L1_ffn_gw2, L1_ffn_gb2):
    C = np.ascontiguousarray
    x = np.asarray(L0_attn_x, dtype=np.float32)

    maps = []
    for c in range(NCORES):
        b, r = c // 2, c % 2
        maps.append({
            "xb": C(x[b]),
            "wq": C(L0_attn_wq_w[r * 2048:(r + 1) * 2048]),
            "wqb": C(L0_attn_wq_b[r * 2048:(r + 1) * 2048].reshape(16, 128).T),
            "wk": C(L0_attn_wk_w), "wkb": C(L0_attn_wk_b.reshape(-1, 1)),
            "wv": C(L0_attn_wv_w), "wvb": C(L0_attn_wv_b.reshape(-1, 1)),
        })
    ra = _run("attn", build_attn, maps)

    maps = []
    for c in range(NCORES):
        b, r = c // 2, c % 2
        ts = slice(r * TME, (r + 1) * TME)
        attn_full = np.concatenate([ra[2 * b]["attn"][:, ts], ra[2 * b + 1]["attn"][:, ts]], axis=0)
        maps.append({
            "attn": C(attn_full),
            "wo": C(L0_attn_wo_w), "wob": C(L0_attn_wo_b.reshape(32, 128).T),
            "xmy": C(x[b, ts]),
        })
    rb = _run("wo", build_wo, maps)

    maps = []
    for c in range(NCORES):
        maps.append({
            "qw": C(L0_ffn_qweight[:, c * 512:(c + 1) * 512]),
            "qz": C(L0_ffn_qzeros[:, c * 64:(c + 1) * 64]),
            "sc": C(L0_ffn_scales[:, c * 512:(c + 1) * 512]),
            "gidx": C(L0_ffn_g_idx.reshape(-1, 1)),
        })
    rc = _run("deq", build_dequant, maps)
    wfull = np.concatenate([rc[c]["wslice"] for c in range(NCORES)], axis=1)

    maps = []
    for c in range(NCORES):
        maps.append({
            "x1": C(rb[c]["x1"]),
            "wfull": wfull,
            "fbias": C(L0_ffn_bias.reshape(32, 128).T),
        })
    rd = _run("ffn", build_ffn, maps)

    x2 = [np.concatenate([rd[2 * b]["x2"], rd[2 * b + 1]["x2"]], axis=1) for b in range(B)]
    maps = []
    for c in range(NCORES):
        b, r = c // 2, c % 2
        ds = slice(r * 2048, (r + 1) * 2048)
        maps.append({
            "xr": C(x2[b][ds]),
            "x2f": x2[b],
            "xproj": C(L1_attn_x_proj_w),
            "dtw": C(L1_attn_dt_proj_w[ds]),
            "dtb": C(L1_attn_dt_proj_b[ds].reshape(16, 128).T),
            "alog": C(L1_attn_A_log[ds]),
            "dvec": C(L1_attn_D[ds].reshape(16, 128).T),
        })
    re_ = _run("ssm", build_ssm, maps)

    zfull = np.concatenate([np.concatenate([re_[2 * b]["z"], re_[2 * b + 1]["z"]], axis=0)
                            for b in range(B)], axis=1)
    loraA_f = C(L1_ffn_lora_A.reshape(8 * D, 16))
    loraB_f = C(L1_ffn_lora_B.reshape(128, D))
    maps = []
    for c in range(NCORES):
        zrot = np.concatenate([zfull[:, c * TME:], zfull[:, :c * TME]], axis=1)
        maps.append({
            "zall": C(zrot),
            "w1": C(L1_ffn_weight),
            "loraA": loraA_f, "loraB": loraB_f,
            "gw1": C(L1_ffn_gw1), "gb1": C(L1_ffn_gb1.reshape(-1, 1)),
            "lng": C(L1_ffn_ln_gamma.reshape(1, -1)), "lnb": C(L1_ffn_ln_beta.reshape(1, -1)),
            "gw2": C(L1_ffn_gw2.T), "gb2": C(L1_ffn_gb2.reshape(-1, 1)),
        })
    rf = _run("final", build_final, maps)

    out = np.empty((B, T, D), np.float32)
    for c in range(NCORES):
        b, r = c // 2, c % 2
        out[b, r * TME:(r + 1) * TME, :] = rf[c]["out"]
    return out


def kernel(L0_attn_x, L0_attn_wq_w, L0_attn_wq_b, L0_attn_wk_w, L0_attn_wk_b,
           L0_attn_wv_w, L0_attn_wv_b, L0_attn_wo_w, L0_attn_wo_b,
           L0_ffn_qweight, L0_ffn_qzeros, L0_ffn_scales, L0_ffn_g_idx, L0_ffn_bias,
           L1_attn_A_log, L1_attn_D, L1_attn_x_proj_w, L1_attn_dt_proj_w, L1_attn_dt_proj_b,
           L1_ffn_weight, L1_ffn_lora_A, L1_ffn_lora_B, L1_ffn_gw1, L1_ffn_gb1,
           L1_ffn_ln_gamma, L1_ffn_ln_beta, L1_ffn_gw2, L1_ffn_gb2):
    C = np.ascontiguousarray
    x = np.asarray(L0_attn_x, dtype=np.float32)
    loraA_f = C(L1_ffn_lora_A.reshape(8 * D, 16))
    loraB_f = C(L1_ffn_lora_B.reshape(128, D))
    DR = D + 160
    maps = []
    for c in range(NCORES):
        b, r = c // 2, c % 2
        ts = slice(r * TME, (r + 1) * TME)
        ds = slice(r * 2048, (r + 1) * 2048)
        f = np.arange(D)
        xidx = ((f // 2048) * 4096 + r * 2048 + (f % 2048)).astype(np.int32)
        kt = np.arange(32)
        p = np.arange(128)
        sidx = ((kt[None, :] // 16) * DR * 1 + 0)
        q = kt // 16
        ktl = kt % 16
        sidx = (q[None, :] * DR + r * 2048 + ktl[None, :] * 128 + p[:, None]).astype(np.int32)
        maps.append({
            "xb": C(x[b]),
            "xmy": C(x[b, ts]),
            "wq": C(L0_attn_wq_w[r * 2048:(r + 1) * 2048]),
            "wqb": C(L0_attn_wq_b[r * 2048:(r + 1) * 2048].reshape(16, 128).T),
            "wk": C(L0_attn_wk_w), "wkb": C(L0_attn_wk_b.reshape(-1, 1)),
            "wv": C(L0_attn_wv_w), "wvb": C(L0_attn_wv_b.reshape(-1, 1)),
            "wo": C(L0_attn_wo_w), "wob": C(L0_attn_wo_b.reshape(32, 128).T),
            "qw": C(L0_ffn_qweight[:, c * 512:(c + 1) * 512]),
            "qz": C(L0_ffn_qzeros[:, c * 64:(c + 1) * 64]),
            "sc": C(L0_ffn_scales[:, c * 512:(c + 1) * 512]),
            "gidx": C(L0_ffn_g_idx.reshape(-1, 1)),
            "fbias": C(L0_ffn_bias.reshape(32, 128).T),
            "xproj": C(L1_attn_x_proj_w),
            "dtw": C(L1_attn_dt_proj_w[ds]),
            "dtb": C(L1_attn_dt_proj_b[ds].reshape(16, 128).T),
            "alog": C(L1_attn_A_log[ds]),
            "dvec": C(L1_attn_D[ds].reshape(16, 128).T),
            "w1": C(L1_ffn_weight),
            "loraA": loraA_f, "loraB": loraB_f,
            "gw1": C(L1_ffn_gw1), "gb1": C(L1_ffn_gb1.reshape(-1, 1)),
            "lng": C(L1_ffn_ln_gamma.reshape(1, -1)), "lnb": C(L1_ffn_ln_beta.reshape(1, -1)),
            "gw2": C(L1_ffn_gw2.T), "gb2": C(L1_ffn_gb2.reshape(-1, 1)),
            "xidx": C(xidx.reshape(32, 128).T),
            "sidx": C(sidx),
        })
    res = _run("mega", build_mega, maps)
    out = np.empty((B, T, D), np.float32)
    for c in range(NCORES):
        b, r = c // 2, c % 2
        out[b, r * TME:(r + 1) * TME, :] = np.asarray(res[c]["out"])
    return out
